# version 04
# baseline (speedup 1.0000x reference)
import numpy as np
import ml_dtypes
from contextlib import ExitStack

import concourse.bass as bass
import concourse.mybir as mybir
from concourse.bass_utils import run_bass_kernel_spmd

F32 = mybir.dt.float32
BF16 = mybir.dt.bfloat16
I32 = mybir.dt.int32
AF = mybir.ActivationFunctionType
ALU = mybir.AluOpType
AX = mybir.AxisListType

P = 128
D = 2048
TOK = 1024
NCH = TOK // P
NFC = D // P
B, S = 4, 2048
H_FFN = 8192
EPS = 1e-5

DT_BYTES = {F32: 4, BF16: 2, I32: 4}


class Sem:
    def __init__(self, kb, name):
        self.h = kb.ctx.enter_context(kb.nc.semaphore(name))
        self.n = 0

    def bump(self, k=1):
        self.n += k
        return self.n


class Kb:
    ENGS = ("sync", "scalar", "vector", "tensor", "gpsimd")

    def __init__(self, nc, ctx):
        self.nc = nc
        self.ctx = ctx
        self.ops = {e: [] for e in self.ENGS}
        self.dbg = {}

    def sem(self, name):
        return Sem(self, name)

    def emit(self, eng, fn):
        self.ops[eng].append(fn)

    def wait(self, eng, sem, val):
        assert val <= sem.n, f"wait on {val} but only {sem.n} scheduled"
        self.emit(eng, lambda e: e.wait_ge(sem.h, val))

    def op(self, eng, build, inc=None, k=1):
        if inc is not None:
            inc.bump(k)
        def f(e):
            ins = build(e)
            if inc is not None:
                ins.then_inc(inc.h, k)
        self.emit(eng, f)
        return inc.n if inc is not None else None

    def dma(self, eng, build, inc):
        return self.op(eng, build, inc=inc, k=16)

    def finalize(self, block):
        for eng in self.ENGS:
            thunks = self.ops[eng]
            getattr(block, eng)(lambda e, th=thunks: [f(e) for f in th])


def build_core(dbg=()):
    dbg = set(dbg)
    nc = bass.Bass("TRN2", target_bir_lowering=False)

    io = {}

    def inp(name, shape, dtype=F32):
        io[name] = nc.dram_tensor(name, list(shape), dtype, kind="ExternalInput").ap()
        return io[name]

    def outp(name, shape, dtype=F32):
        io[name] = nc.dram_tensor(name, list(shape), dtype, kind="ExternalOutput").ap()
        return io[name]

    xs = inp("xs", [TOK, D])
    consts = inp("consts", [P, 640])
    constb = inp("constb", [P, 512], BF16)
    out_x = outp("out_x", [TOK, D])

    ctx = ExitStack()
    with ctx:
        kb = Kb(nc, ctx)
        e = ctx.enter_context

        X = e(nc.sbuf_tensor("X", [P, NCH * D], F32))
        XNT = e(nc.sbuf_tensor("XNT", [P, NFC * TOK], BF16))
        ST = e(nc.sbuf_tensor("ST", [P, 160], F32))
        CONS = e(nc.sbuf_tensor("CONS", [P, 640], F32))
        CONB = e(nc.sbuf_tensor("CONB", [P, 512], BF16))
        AF32 = e(nc.sbuf_tensor("AF32", [P, 6944], F32))
        AB16 = e(nc.sbuf_tensor("AB16", [P, 38912], BF16))

        PSF = e(nc.psum_tensor("PSF", [P, 2048], F32))
        PSB = e(nc.psum_tensor("PSB", [P, 2048], BF16))
        PSF2 = e(nc.psum_tensor("PSF2", [P, 1024], F32))

        sd = kb.sem("sd")
        ad = kb.sem("ad")
        gd = kb.sem("gd")
        pe = kb.sem("pe")
        act = kb.sem("act")
        dve = kb.sem("dve")
        pool = kb.sem("pool")
        sto = kb.sem("sto")

        kb.dma("sync", lambda e: e.dma_start(
            out=X[:].rearrange("p (c f) -> p c f", f=D),
            in_=xs.rearrange("(c p) f -> p c f", p=P)), inc=sd)
        kb.dma("sync", lambda e: e.dma_start(out=CONS[:], in_=consts), inc=sd)
        kb.dma("sync", lambda e: e.dma_start(out=CONB[:], in_=constb), inc=sd)
        ld_done = sd.n

        ID32 = CONS[:, 0:128]
        EPSC = CONS[:, 576:577]
        IDB = CONB[:, 0:128]
        PERM = CONB[:, 128:256]
        ONEB = CONB[:, 256:257]

        def ln_phase(L, after=None):
            if after is not None:
                for eng in ("sync", "scalar", "vector", "tensor", "gpsimd"):
                    kb.wait(eng, dve, after["done_dve"])
                    kb.wait(eng, act, after["done_act"])
                    kb.wait(eng, pe, after["done_pe"])
                    if pool.n:
                        kb.wait(eng, pool, pool.n)
            c_sum, c_ssq, c_m, c_var, c_sd, c_rstd, c_nm, c_m2 = 0, 8, 16, 24, 32, 40, 48, 56
            SQ = AF32[:, 0:2048]
            CENT0 = AF32[:, 2048:4096]
            CENT1 = AF32[:, 4096:6144]
            CENT = [CENT0, CENT1]
            XNTM0 = AB16[:, 0:2048]
            XNTM1 = AB16[:, 2048:4096]
            XNTM = [XNTM0, XNTM1]

            kb.wait("vector", sd, ld_done)
            kb.wait("scalar", sd, ld_done)

            a_sums = []
            a_ssq = []
            for c in range(NCH):
                Xc = X[:, c * D:(c + 1) * D]
                a_sums.append(kb.op("vector", lambda e, Xc=Xc, c=c: e.reduce_sum(
                    ST[:, c_sum + c:c_sum + c + 1], Xc, axis=AX.X), inc=dve))
                a_ssq.append(kb.op("scalar", lambda e, Xc=Xc, c=c: e.activation(
                    SQ, Xc, AF.Square,
                    accum_out=ST[:, c_ssq + c:c_ssq + c + 1]), inc=act))

            def stc(col, c):
                return ST[:, col + c:col + c + 1]

            c_t = 64
            kb.wait("vector", act, a_ssq[-1])
            kb.op("vector", lambda e: e.tensor_scalar(
                ST[:, c_m:c_m + NCH], ST[:, c_sum:c_sum + NCH],
                1.0 / D, None, ALU.mult), inc=dve)
            kb.op("vector", lambda e: e.tensor_scalar(
                ST[:, c_nm:c_nm + NCH], ST[:, c_sum:c_sum + NCH],
                -1.0 / D, None, ALU.mult), inc=dve)
            kb.op("vector", lambda e: e.tensor_scalar(
                ST[:, c_t:c_t + NCH], ST[:, c_ssq:c_ssq + NCH],
                1.0 / D, None, ALU.mult), inc=dve)
            b1 = dve.n
            kb.wait("scalar", dve, b1)
            kb.op("scalar", lambda e: e.activation(
                ST[:, c_m2:c_m2 + NCH], ST[:, c_m:c_m + NCH], AF.Square), inc=act)
            b2 = act.n
            kb.wait("vector", act, b2)
            kb.op("vector", lambda e: e.tensor_sub(
                ST[:, c_var:c_var + NCH], ST[:, c_t:c_t + NCH],
                ST[:, c_m2:c_m2 + NCH]), inc=dve)
            b3 = dve.n
            kb.wait("scalar", dve, b3)
            kb.op("scalar", lambda e: e.activation(
                ST[:, c_sd:c_sd + NCH], ST[:, c_var:c_var + NCH],
                AF.Sqrt, bias=EPSC), inc=act)
            b4 = act.n
            kb.wait("vector", act, b4)
            kb.op("vector", lambda e: e.reciprocal(
                ST[:, c_rstd:c_rstd + NCH], ST[:, c_sd:c_sd + NCH]), inc=dve)
            b_rstd = dve.n

            ev_cent = {}
            ev_xn = {}
            ev_tr = {}
            ev_evac = {}
            dbg_gate = {}
            kb.wait("scalar", dve, b_rstd)
            for c in range(NCH):
                par = c % 2
                Xc = X[:, c * D:(c + 1) * D]
                nm = stc(c_nm, c)
                rstd = stc(c_rstd, c)
                if c >= 2:
                    kb.wait("scalar", dve, ev_xn[c - 2])
                if c == 2 and dbg_gate:
                    kb.wait("vector", sto, dbg_gate["sto"])
                ev_cent[c] = kb.op("scalar", lambda e, Xc=Xc, nm=nm, d=CENT[par]:
                    e.activation(d, Xc, AF.Identity, bias=nm), inc=act)
                kb.wait("vector", act, ev_cent[c])
                if c >= 2:
                    kb.wait("vector", pe, ev_tr[c - 2])
                ev_xn[c] = kb.op("vector", lambda e, d=XNTM[par], s=CENT[par],
                    rstd=rstd: e.tensor_scalar(d, s, rstd, None, ALU.mult),
                    inc=dve)
                if c == 0 and L == 0 and "xn0" in dbg:
                    dxn0 = outp("dbg_xn0", [P, D], BF16)
                    kb.wait("sync", dve, ev_xn[0])
                    kb.dma("sync", lambda e, d=dxn0: e.dma_start(out=d, in_=XNTM0),
                           inc=sto)
                    dbg_gate["sto"] = sto.n
                kb.wait("tensor", dve, ev_xn[c])
                if c >= 1:
                    kb.wait("tensor", dve, ev_evac[c - 1])
                for fc in range(NFC):
                    last = fc == NFC - 1
                    kb.op("tensor", lambda e, d=PSB[:, fc * 128:(fc + 1) * 128],
                        s=XNTM[par][:, fc * 128:(fc + 1) * 128]:
                        e.transpose(d, s, IDB),
                        inc=pe if last else None)
                ev_tr[c] = pe.n
                kb.wait("vector", pe, ev_tr[c])
                dst = XNT[:].rearrange("p (f t) -> p f t", f=NFC)[:, :, c * 128:(c + 1) * 128]
                src = PSB[:].rearrange("p (f t) -> p f t", f=NFC)
                ev_evac[c] = kb.op("vector", lambda e, dst=dst, src=src:
                    e.tensor_copy(dst, src), inc=dve)
            return dict(done_pe=pe.n, done_dve=dve.n, done_act=act.n)

        ln0 = ln_phase(0)

        NH0 = 16
        cq = inp("L0q_codes", [D, D])
        ck = inp("L0k_codes", [D, D])
        cv = inp("L0v_codes", [D, D])
        co = inp("L0o_codes", [D, D])
        sq = inp("L0q_scp", [16, 4, D])
        sk = inp("L0k_scp", [16, 4, D])
        sv = inp("L0v_scp", [16, 4, D])
        so = inp("L0o_scp", [16, 4, D])
        bl0 = inp("L0_bias", [P, 64])

        kv_in = nc.dram_tensor("kv_in", [2 * D, TOK], BF16, kind="Internal").ap()
        kv_out = nc.dram_tensor("kv_out", [2, 2 * D, TOK], BF16, kind="Internal").ap()

        ATT = AB16[:, 0:16384]
        WPC = [AB16[:, 16384:18432], AB16[:, 18432:20480]]
        KVST = [AB16[:, 20480:21504], AB16[:, 21504:22528]]
        QTH = [AB16[:, 22528:23552], AB16[:, 23552:24576]]
        KTH = AB16[:, 24576:26624]
        VTR = AB16[:, 26624:28672]
        VTL = AB16[:, 28672:30720]
        PB = AB16[:, 30720:32768]
        PTG = AB16[:, 32768:36864]
        SCB = AB16[0:4, 36864:38912]
        CSTG = [AF32[:, 0:2048], AF32[:, 2048:4096]]
        SCP = AF32[0:4, 4096:6144]
        E4B = CONB[0:4, 384:512]
        GAPC = ST[:, 158:159]

        def gap(eng):
            kb.op(eng, lambda e: e.tensor_scalar(GAPC, GAPC, 1.0, None, ALU.mult))

        def barrier():
            for eng in ("tensor", "scalar", "vector", "gpsimd", "sync"):
                for s in (pe, act, dve, pool, sd, ad, gd):
                    if s.n:
                        kb.wait(eng, s, s.n)

        locks = {"PSF2": None, "PSB": None, "PSF0": None, "PSF1": None}

        def lk_acquire(name):
            if locks[name] is not None:
                s, v = locks[name]
                kb.wait("tensor", s, v)

        def lk_release(name, s, v):
            locks[name] = (s, v)

        BIAS = e(nc.sbuf_tensor("BIAS", [P, 80], F32))
        kb.dma("sync", lambda e: e.dma_start(out=BIAS[:, 0:64], in_=bl0), inc=sd)
        bias_ld = sd.n
        kb.wait("vector", sd, bias_ld)
        kb.op("vector", lambda e: e.tensor_scalar(
            BIAS[:, 64:80], BIAS[:, 0:16], float(1.0 / np.sqrt(128.0)),
            None, ALU.mult), inc=dve)

        def l0_attn():
            kb.wait("sync", act, ln0["done_act"])
            kb.wait("sync", dve, ln0["done_dve"])
            scv = {"q": sq, "k": sk, "v": sv, "o": so}
            cov = {"q": cq, "k": ck, "v": cv, "o": co}
            wp_ready = {}
            wp_done = {}
            wp_exp_pe = {}
            kvst_free = [(None, 0), (None, 0)]
            pidx = [0]
            scb_cast = [0]

            def prep_piece(kind, pc):
                i = pidx[0]
                pidx[0] += 1
                par = i % 2
                if i >= 2:
                    kb.wait("sync", dve, wp_ready[i - 2])
                if i >= 1:
                    kb.wait("sync", dve, scb_cast[0])
                src = cov[kind].rearrange("(dc p) f -> p dc f", p=P)[:, :, pc * 128:(pc + 1) * 128]
                dst = CSTG[par].rearrange("p (dc f) -> p dc f", f=128)
                kb.dma("sync", lambda e, s=src, d=dst: e.dma_start(out=d, in_=s), inc=sd)
                kb.dma("sync", lambda e, s=scv[kind][pc], d=SCP:
                    e.dma_start(out=d, in_=s), inc=sd)
                ld = sd.n
                kb.wait("vector", sd, ld)
                if i >= 1:
                    kb.wait("vector", pe, wp_exp_pe[i - 1])
                kb.op("vector", lambda e: e.tensor_copy(SCB, SCP), inc=dve)
                scb_cast[0] = dve.n
                scb_ready = dve.n
                kb.wait("tensor", dve, scb_ready)
                if i >= 2 and wp_done.get(i - 2) is not None:
                    pass
                for rnd in range(2):
                    lk_acquire("PSF2")
                    for j in range(8):
                        dc = rnd * 8 + j
                        kb.op("tensor", lambda e, j=j, dc=dc, par=par:
                            e.matmul(PSF2[:, j * 128:(j + 1) * 128], E4B,
                                     SCB[:, dc * 128:(dc + 1) * 128],
                                     start=True, stop=True),
                            inc=pe if j == 7 else None)
                    tr = pe.n
                    kb.wait("vector", pe, tr)
                    if rnd == 0 and i >= 2 and wp_done.get(i - 2) is not None:
                        kb.wait("vector", pe, wp_done[i - 2])
                    kb.op("vector", lambda e, rnd=rnd, par=par:
                        e.tensor_mul(WPC[par][:, rnd * 1024:(rnd + 1) * 1024],
                                     CSTG[par][:, rnd * 1024:(rnd + 1) * 1024],
                                     PSF2[:]), inc=dve)
                    lk_release("PSF2", dve, dve.n)
                wp_exp_pe[i] = pe.n
                wp_ready[i] = dve.n
                return i, par

            def piece_matmuls(i, par, rhs, psf_half, bias_col, stage, sp,
                              scale=1.0):
                lname = "PSF0" if psf_half == 0 else "PSF1"
                kb.wait("tensor", dve, wp_ready[i])
                lk_acquire(lname)
                dstp = PSF[:, psf_half * 1024:(psf_half + 1) * 1024]
                for dc in range(NFC):
                    for nh in range(2):
                        kb.op("tensor", lambda e, dc=dc, nh=nh, par=par,
                            d=dstp[:, nh * 512:(nh + 1) * 512], r=rhs(dc, nh):
                            e.matmul(d, WPC[par][:, dc * 128:(dc + 1) * 128], r,
                                     start=(dc == 0), stop=(dc == NFC - 1)),
                            inc=pe if (dc == NFC - 1 and nh == 1) else None)
                wp_done[i] = pe.n
                mm = pe.n
                kb.wait("scalar", pe, mm)
                fs, fv = kvst_free[sp]
                if fs is not None:
                    kb.wait("scalar", fs, fv)
                kb.op("scalar", lambda e, d=stage, s=dstp, b=bias_col:
                    e.activation(d, s, AF.Identity, bias=b, scale=scale), inc=act)
                ev = act.n
                lk_release(lname, act, ev)
                return ev

            def xnt_rhs(dc, nh):
                return XNT[:, dc * 1024 + nh * 512:dc * 1024 + (nh + 1) * 512]

            def att_rhs(dc, nh):
                return ATT[:, dc * 1024 + nh * 512:dc * 1024 + (nh + 1) * 512]

            kb.wait("scalar", sd, bias_ld)
            prod_ev = {0: None, 1: None}
            for kind, row0, bcol in (("k", 0, 16), ("v", D, 32)):
                for pc in range(NFC):
                    i, par = prep_piece(kind, pc)
                    sp = i % 2
                    ev = piece_matmuls(i, par, xnt_rhs, i % 2,
                                       BIAS[:, bcol + pc:bcol + pc + 1], KVST[sp], sp)
                    prod_ev[i % 2] = ev
                    kb.wait("gpsimd", act, ev)
                    kb.dma("gpsimd", lambda e,
                        d=kv_in[row0 + pc * 128:row0 + (pc + 1) * 128, :],
                        s=KVST[sp]: e.dma_start(out=d, in_=s), inc=gd)
                    kvst_free[sp] = (gd, gd.n)
            cc = kb.sem("cc0")
            kb.wait("gpsimd", gd, gd.n)
            kb.op("gpsimd", lambda e: e.collective_compute(
                "AllGather", ALU.bypass,
                replica_groups=[[0, 1], [2, 3], [4, 5], [6, 7]],
                ins=[kv_in], outs=[kv_out]), inc=cc)

            if "kvout" in dbg:
                dkv = outp("dbg_kvout", [2, 2 * D, TOK], BF16)
                kb.wait("sync", cc, 1)
                kb.dma("sync", lambda e, d=dkv: e.dma_start(out=d, in_=kv_out),
                       inc=sto)

            ev_exp = {}
            ev_ptev = {}
            ev_qth = {}
            sc_last = {}
            vtr_pe = {}
            pend_tr = []
            ev_vtl = {}
            av_pe_last = [0]
            qth_dbg_gate = [0]
            pb_dbg_gate = [0]
            ev_comb = {}
            PBX = [PB, VTR]
            for b in range(4):
                ev_exp[(-1, NCH - 1, b)] = prod_ev[0] if b < 2 else prod_ev[1]

            def emit_transposes(h, qc):
                hp = h % 2
                pbuf = PBX[qc % 2]
                kb.wait("tensor", pool, ev_ptev[("norm", h, qc)])
                lk_acquire("PSB")
                for kc in range(16):
                    kb.op("tensor", lambda e, kc=kc, pbuf=pbuf:
                        e.transpose(PSB[:, kc * 128:(kc + 1) * 128],
                                    pbuf[:, kc * 128:(kc + 1) * 128], IDB),
                        inc=pe if kc == 15 else None)
                trp = pe.n
                kb.wait("vector", pe, trp)
                if av_pe_last[0]:
                    kb.wait("vector", pe, av_pe_last[0])
                dstp = PTG.rearrange("p (kc q) -> p kc q", q=256)[:, :, (qc % 2) * 128:(qc % 2) * 128 + 128]
                srcp = PSB[:].rearrange("p (kc q) -> p kc q", q=128)
                kb.op("vector", lambda e, d=dstp, s=srcp:
                    e.tensor_copy(d, s), inc=dve)
                ev_ptev[(h, qc)] = (dve.n, trp)
                lk_release("PSB", dve, dve.n)
                if qc % 2 == 1:
                    g = qc // 2
                    kb.wait("tensor", dve, ev_ptev[(h, qc)][0])
                    kb.wait("tensor", dve, ev_vtl[h])
                    lk_acquire("PSF2")
                    for kc in range(16):
                        kb.op("tensor", lambda e, kc=kc:
                            e.matmul(PSF2[:, 0:256],
                                     VTL[:, kc * 128:(kc + 1) * 128],
                                     PTG[:, kc * 256:(kc + 1) * 256],
                                     start=(kc == 0), stop=(kc == 15)),
                            inc=pe if kc == 15 else None)
                    av = pe.n
                    av_pe_last[0] = av
                    kb.wait("vector", pe, av)
                    kb.op("vector", lambda e, h=h, g=g: e.tensor_copy(
                        ATT[:, h * 1024 + g * 256:h * 1024 + (g + 1) * 256],
                        PSF2[:, 0:256]), inc=dve)
                    lk_release("PSF2", dve, dve.n)
                    if h == 0 and g == 0 and "att00" in dbg:
                        da0 = outp("dbg_att00", [P, 256], BF16)
                        dpt0 = outp("dbg_ptg0", [P, 4096], BF16)
                        dvt0 = outp("dbg_vtl0", [P, 2048], BF16)
                        kb.wait("sync", dve, dve.n)
                        kb.dma("sync", lambda e, d=da0: e.dma_start(
                            out=d, in_=ATT[:, 0:256]), inc=sto)
                        kb.dma("sync", lambda e, d=dpt0: e.dma_start(
                            out=d, in_=PTG), inc=sto)
                        kb.dma("sync", lambda e, d=dvt0: e.dma_start(
                            out=d, in_=VTL), inc=sto)
                        pb_dbg_gate[0] = sto.n

            for h in range(NH0):
                hp = h % 2
                i, par = prep_piece("q", h)
                kb.wait("tensor", dve, wp_ready[i])
                for rnd in range(2):
                    lk_acquire("PSF2")
                    for dc in range(NFC):
                        kb.op("tensor", lambda e, dc=dc, rnd=rnd, par=par:
                            e.matmul(PSF2[:, rnd * 512:(rnd + 1) * 512],
                                     WPC[par][:, dc * 128:(dc + 1) * 128],
                                     XNT[:, dc * 1024 + rnd * 512:dc * 1024 + (rnd + 1) * 512],
                                     start=(dc == 0), stop=(dc == NFC - 1)),
                            inc=pe if dc == NFC - 1 else None)
                    wp_done[i] = pe.n
                    kb.wait("scalar", pe, pe.n)
                    if h >= 2 and rnd == 0:
                        kb.wait("scalar", pe, sc_last[h - 2])
                    kb.op("scalar", lambda e, rnd=rnd, hp=hp, h=h:
                        e.activation(QTH[hp][:, rnd * 512:(rnd + 1) * 512],
                                     PSF2[:, rnd * 512:(rnd + 1) * 512],
                                     AF.Identity, bias=BIAS[:, 64 + h:65 + h],
                                     scale=float(1.0 / np.sqrt(128.0))), inc=act)
                    lk_release("PSF2", act, act.n)
                ev_qth[h] = act.n
                if h == 0 and "qth0" in dbg:
                    dqt = outp("dbg_qth0", [P, TOK], BF16)
                    kb.wait("sync", act, ev_qth[0])
                    kb.dma("sync", lambda e, d=dqt: e.dma_start(out=d, in_=QTH[0]),
                           inc=sto)
                    qth_dbg_gate[0] = sto.n
                if h == 2 and qth_dbg_gate[0]:
                    kb.wait("scalar", sto, qth_dbg_gate[0])

                if h == 0:
                    kb.wait("scalar", cc, 1)
                else:
                    kb.wait("scalar", pe, sc_last[h - 1])
                    kb.wait("scalar", pe, ev_ptev[(h - 1, NCH - 1)][1])
                src = kv_out[:, h * 128:(h + 1) * 128, :].rearrange("r p t -> p r t")
                kb.dma("scalar", lambda e, s=src: e.dma_start(
                    out=KTH[:].rearrange("p (r t) -> p r t", r=2), in_=s), inc=ad)
                srcv = kv_out[:, D + h * 128:D + (h + 1) * 128, :].rearrange("r p t -> p r t")
                kb.dma("scalar", lambda e, s=srcv: e.dma_start(
                    out=VTR[:].rearrange("p (r t) -> p r t", r=2), in_=s), inc=ad)
                ld_kv = ad.n

                kb.wait("tensor", ad, ld_kv)
                lk_acquire("PSB")
                for kc in range(16):
                    kb.op("tensor", lambda e, kc=kc:
                        e.transpose(PSB[:, kc * 128:(kc + 1) * 128],
                                    VTR[:, kc * 128:(kc + 1) * 128], IDB),
                        inc=pe if kc == 15 else None)
                vtr_pe[h] = pe.n
                kb.wait("vector", pe, pe.n)
                if av_pe_last[0]:
                    kb.wait("vector", pe, av_pe_last[0])
                kb.op("vector", lambda e: e.tensor_copy(VTL, PSB[:]), inc=dve)
                ev_vtl[h] = dve.n
                lk_release("PSB", dve, dve.n)

                for qc in range(NCH):
                    if "ser" in dbg:
                        barrier()
                    kb.wait("tensor", act, ev_qth[h])
                    if qc == 0:
                        kb.wait("tensor", ad, ld_kv)
                    if pb_dbg_gate[0] and (h, qc) != (0, 0):
                        kb.wait("scalar", sto, pb_dbg_gate[0])
                        pb_dbg_gate[0] = 0
                    for b in range(4):
                        prev = (h, qc - 1, b) if qc >= 1 else (h - 1, NCH - 1, b)
                        if prev in ev_exp:
                            kb.wait("tensor", act, ev_exp[prev])
                        kb.op("tensor", lambda e, b=b, hp=hp, qc=qc:
                            e.matmul(PSF[:, b * 512:(b + 1) * 512],
                                     QTH[hp][:, qc * 128:(qc + 1) * 128],
                                     KTH[:, b * 512:(b + 1) * 512],
                                     start=True, stop=True), inc=pe)
                        mmb = pe.n
                        kb.wait("scalar", pe, mmb)
                        prevt = (h, qc - 2) if qc >= 2 else (h - 1, NCH - 2 + qc)
                        if prevt in ev_ptev and b == 0:
                            kb.wait("scalar", pe, ev_ptev[prevt][1])
                        prevc = (h, qc - 1) if qc >= 1 else (h - 1, NCH - 1)
                        if prevc in ev_comb and b == 0:
                            kb.wait("scalar", dve, ev_comb[prevc])
                        if qc == 1 and b == 0:
                            kb.wait("scalar", pe, vtr_pe[h])
                        kb.op("scalar", lambda e, b=b, pbuf=PBX[qc % 2]:
                            e.activation(pbuf[:, b * 512:(b + 1) * 512],
                                         PSF[:, b * 512:(b + 1) * 512], AF.Exp,
                                         accum_out=ST[:, 96 + b:97 + b]), inc=act)
                        ev_exp[(h, qc, b)] = act.n
                    sc_last[h] = pe.n
                    kb.wait("vector", act, ev_exp[(h, qc, 3)])
                    prevt = (h, qc - 1) if qc >= 1 else (h - 1, NCH - 1)
                    if prevt in ev_ptev:
                        kb.wait("vector", pe, ev_ptev[prevt][1])
                    kb.op("vector", lambda e: e.reduce_sum(
                        ST[:, 100:101], ST[:, 96:100], axis=AX.X), inc=dve)
                    ev_comb[(h, qc)] = dve.n
                    gap("vector")
                    jj = h * NCH + qc
                    rpar = jj % 2
                    if ("rcol", rpar) in ev_ptev:
                        kb.wait("vector", pool, ev_ptev[("rcol", rpar)])
                    kb.op("vector", lambda e, rp=rpar: e.reciprocal(
                        ST[:, 101 + rp:102 + rp], ST[:, 100:101]), inc=dve)
                    rc = dve.n
                    kb.wait("gpsimd", dve, rc)
                    kb.wait("gpsimd", act, ev_exp[(h, qc, 3)])
                    kb.op("gpsimd", lambda e, rp=rpar, pbuf=PBX[qc % 2]:
                        e.tensor_scalar(
                        pbuf[:], pbuf[:], ST[:, 101 + rp:102 + rp], None, ALU.mult),
                        inc=pool)
                    ev_ptev[("norm", h, qc)] = pool.n
                    ev_ptev[("rcol", rpar)] = pool.n
                    if h == 0 and qc == 0 and "pb0" in dbg:
                        dpb = outp("dbg_pb0", [P, 2048], BF16)
                        dstx2 = outp("dbg_sums0", [P, 8])
                        kb.wait("sync", dve, dve.n)
                        kb.wait("sync", pool, ev_ptev[("norm", 0, 0)])
                        kb.dma("sync", lambda e, d=dpb: e.dma_start(out=d, in_=PB[:]),
                               inc=sto)
                        kb.dma("sync", lambda e, d=dstx2: e.dma_start(
                            out=d, in_=ST[:, 96:104]), inc=sto)
                        pb_dbg_gate[0] = sto.n
                    pend_tr.append((h, qc))
                    if len(pend_tr) >= 2:
                        emit_transposes(*pend_tr.pop(0))
                while pend_tr:
                    emit_transposes(*pend_tr.pop(0))

            last_exp = ev_exp[(NH0 - 1, NCH - 1, 3)]
            lk_release("PSF0", act, last_exp)
            lk_release("PSF1", act, last_exp)
            for pc in range(NFC):
                i, par = prep_piece("o", pc)
                sp = i % 2
                ev = piece_matmuls(i, par, att_rhs, i % 2,
                                   BIAS[:, 48 + pc:48 + pc + 1], KVST[sp], sp)
                kb.wait("tensor", act, ev)
                lk_acquire("PSB")
                for tc in range(NCH):
                    kb.op("tensor", lambda e, tc=tc, sp=sp:
                        e.transpose(PSB[:, tc * 128:(tc + 1) * 128],
                                    KVST[sp][:, tc * 128:(tc + 1) * 128], IDB),
                        inc=pe if tc == NCH - 1 else None)
                tro = pe.n
                kvst_free[sp] = (pe, tro)
                kb.wait("vector", pe, tro)
                dstx = X[:].rearrange("p (c f) -> p c f", f=D)[:, :, pc * 128:(pc + 1) * 128]
                srcx = PSB[:, 0:1024].rearrange("p (c f) -> p c f", f=128)
                kb.op("vector", lambda e, d=dstx, s=srcx:
                    e.tensor_add(d, d, s), inc=dve)
                lk_release("PSB", dve, dve.n)
            return dict(done_dve=dve.n, done_act=act.n, done_pe=pe.n)

        ph2 = l0_attn()

        if "att" in dbg:
            datt = outp("dbg_att", [P, 16 * TOK], BF16)
            kb.wait("sync", dve, ph2["done_dve"])
            kb.dma("sync", lambda e, d=datt: e.dma_start(out=d, in_=ATT), inc=sto)

        cfc_t = inp("cfc_t", [16, D, 512])
        cpj_t = inp("cpj_t", [64, P, D])
        q1d = nc.dram_tensor("q1d", [16, P, 8192], BF16, kind="Internal").ap()
        AI32 = e(nc.sbuf_tensor("AI32", [P, 1024], I32))

        def l0_ffn():
            Q1S = [AB16[:, 0:8192], AB16[:, 8192:16384]]
            Q2S = [AB16[:, 16384:24576], AB16[:, 24576:32768]]
            RB16 = AB16[:, 32768:33792]
            RT16 = AB16[:, 33792:34816]
            WST = [AF32[:, 0:2048], AF32[:, 2048:4096]]
            T1 = [AF32[:, 4096:4608], AF32[:, 4608:5120]]
            TMPX = AF32[:, 5120:6144]
            ONE1 = CONS[:, 320:321]
            ONER = CONS[0:1, 320:448]
            sa1, sa2, tmpa = 102, 103, 104
            s1c, s2c, rs1c, rs2c, s1sq = 106, 107, 108, 109, 119
            hx0 = 110
            rtc0 = 120
            sc20 = 128
            m2c0 = 140
            xsq0 = 148

            def stcol(c0, k=1):
                return ST[:, c0:c0 + k]

            kb.wait("sync", dve, ph2["done_dve"])
            kb.wait("sync", act, ph2["done_act"])
            kb.wait("sync", pe, ph2["done_pe"])
            kb.wait("vector", act, ph2["done_act"])
            kb.wait("vector", pe, ph2["done_pe"])
            kb.wait("scalar", dve, ph2["done_dve"])
            kb.wait("scalar", pe, ph2["done_pe"])
            kb.wait("tensor", dve, ph2["done_dve"])
            kb.wait("tensor", act, ph2["done_act"])
            kb.op("vector", lambda e: e.memset(ST[:, sa1:sa1 + 2], 0.0), inc=dve)
            kb.op("vector", lambda e: e.memset(stcol(hx0, NCH), -1e30), inc=dve)

            ld_ev = {}
            for mi in range(2):
                for p in range(64):
                    par = p % 2
                    gi = mi * 64 + p
                    if gi >= 2:
                        kb.wait("sync", dve, ld_ev[gi - 2])
                    if mi == 0:
                        s0, sub = p // 4, p % 4
                        sap = cfc_t[s0, sub * 512:(sub + 1) * 512, :].rearrange(
                            "(dc q) h -> q dc h", q=P)
                        dap = WST[par].rearrange("q (dc h) -> q dc h", h=512)
                    else:
                        sap = cpj_t[p]
                        dap = WST[par]
                    kb.dma("sync", lambda e, s=sap, d=dap:
                        e.dma_start(out=d, in_=s), inc=sd)
                    ldn = sd.n
                    kb.wait("vector", sd, ldn)
                    kb.op("vector", lambda e, par=par: e.reduce_sum(
                        stcol(tmpa + par), WST[par], axis=AX.X,
                        apply_absolute_value=True), inc=dve)
                    gap("vector")
                    kb.op("vector", lambda e, par=par, acc=sa1 + mi: e.tensor_add(
                        stcol(acc), stcol(acc), stcol(tmpa + par)), inc=dve)
                    ld_ev[gi] = dve.n

            abs_done = dve.n
            kb.wait("tensor", dve, abs_done)
            kb.op("tensor", lambda e: e.matmul(
                PSF2[0:1, 0:1], ST[:, sa1:sa1 + 1], ONE1,
                start=True, stop=True))
            kb.op("tensor", lambda e: e.matmul(
                PSF2[0:1, 1:2], ST[:, sa2:sa2 + 1], ONE1,
                start=True, stop=True), inc=pe)
            t_tot = pe.n
            kb.wait("vector", pe, t_tot)
            kb.op("vector", lambda e: e.tensor_copy(CONS[0:1, 578:580],
                                                    PSF2[0:1, 0:2]), inc=dve)
            tot_ev = dve.n
            kb.wait("tensor", dve, tot_ev)
            kb.op("tensor", lambda e: e.matmul(
                PSF2[:, 0:2], ONER, CONS[0:1, 578:580],
                start=True, stop=True), inc=pe)
            t_bc = pe.n
            kb.wait("vector", pe, t_bc)
            kb.op("vector", lambda e: e.tensor_scalar(
                stcol(s1c, 2), PSF2[:, 0:2], 1.0 / (D * H_FFN), EPS,
                ALU.mult, ALU.max), inc=dve)
            sv = dve.n
            kb.wait("scalar", dve, sv)
            kb.op("scalar", lambda e: e.activation(
                stcol(s1sq), stcol(s1c), AF.Square), inc=act)
            sq_ev = act.n
            kb.wait("vector", act, sq_ev)
            kb.op("vector", lambda e: e.reciprocal(stcol(rs1c), stcol(s1c)),
                  inc=dve)
            gap("vector")
            kb.op("vector", lambda e: e.reciprocal(stcol(rs2c), stcol(s2c)),
                  inc=dve)
            s_ready = dve.n

            aic_ev = [0, 0, 0, 0]

            def quant2048(wst_par, rscol, dst):
                kb.op("vector", lambda e, w=WST[wst_par], rs=stcol(rscol):
                    e.tensor_scalar(w, w, rs, None, ALU.mult), inc=dve)
                mu = dve.n
                kb.wait("scalar", dve, mu)
                for r in range(4):
                    rr = r % 2
                    ai = AI32[:, rr * 512:(rr + 1) * 512]
                    if aic_ev[rr]:
                        kb.wait("scalar", dve, aic_ev[rr])
                    kb.op("scalar", lambda e, ai=ai, w=WST[wst_par], r=r:
                        e.activation(ai, w[:, r * 512:(r + 1) * 512],
                                     AF.Identity), inc=act)
                    an = act.n
                    kb.wait("vector", act, an)
                    kb.op("vector", lambda e, ai=ai, d=dst, r=r:
                        e.tensor_scalar(d[:, r * 512:(r + 1) * 512], ai,
                                        -1.0, 1.0, ALU.max, ALU.min), inc=dve)
                    aic_ev[rr] = dve.n
                return dve.n

            kb.wait("sync", dve, s_ready)
            q1_ready = {}
            mmA_ev = {}
            redA_ev = {}
            q1d_st = {}
            for s in range(16):
                par = s % 2
                for sub in range(4):
                    spar = sub % 2
                    if (s, sub - 2) in q1_ready:
                        kb.wait("sync", dve, q1_ready[(s, sub - 2)])
                    elif s >= 1 and (s - 1, sub + 2) in q1_ready:
                        kb.wait("sync", dve, q1_ready[(s - 1, sub + 2)])
                    sap = cfc_t[s, sub * 512:(sub + 1) * 512, :].rearrange(
                        "(dc q) h -> q dc h", q=P)
                    kb.dma("sync", lambda e, s_=sap,
                        d=WST[spar].rearrange("q (dc h) -> q dc h", h=512):
                        e.dma_start(out=d, in_=s_), inc=sd)
                    kb.wait("vector", sd, sd.n)
                    if sub == 0 and (s - 2) in mmA_ev:
                        kb.wait("vector", pe, mmA_ev[s - 2])
                        kb.wait("vector", gd, q1d_st[s - 2])
                    q1_ready[(s, sub)] = quant2048(
                        spar, rs1c, Q1S[par][:, sub * 2048:(sub + 1) * 2048])
                kb.wait("gpsimd", dve, q1_ready[(s, 3)])
                kb.dma("gpsimd", lambda e, d=q1d[s], s_=Q1S[par]:
                    e.dma_start(out=d, in_=s_), inc=gd)
                q1d_st[s] = gd.n
                for tc in range(NCH):
                    tp = tc % 2
                    kb.wait("tensor", dve, q1_ready[(s, 3)])
                    if (s, tc - 2) in redA_ev:
                        kb.wait("tensor", dve, redA_ev[(s, tc - 2)])
                    elif s >= 1 and (s - 1, tc + 6) in redA_ev:
                        kb.wait("tensor", dve, redA_ev[(s - 1, tc + 6)])
                    for dc in range(NFC):
                        kb.op("tensor", lambda e, dc=dc, tc=tc, tp=tp, par=par:
                            e.matmul(PSF2[:, tp * 512:(tp + 1) * 512],
                                     XNT[:, dc * 1024 + tc * 128:dc * 1024 + (tc + 1) * 128],
                                     Q1S[par][:, dc * 512:(dc + 1) * 512],
                                     start=(dc == 0), stop=(dc == NFC - 1)),
                            inc=pe if dc == NFC - 1 else None)
                    mmA_ev[s] = pe.n
                    kb.wait("vector", pe, pe.n)
                    kb.op("vector", lambda e, tp=tp: e.reduce_max(
                        stcol(tmpa + tp), PSF2[:, tp * 512:(tp + 1) * 512],
                        axis=AX.X), inc=dve)
                    gap("vector")
                    kb.op("vector", lambda e, tp=tp, tc=tc: e.tensor_max(
                        stcol(hx0 + tc), stcol(hx0 + tc), stcol(tmpa + tp)),
                        inc=dve)
                    redA_ev[(s, tc)] = dve.n

            hmax_done = dve.n
            kb.op("vector", lambda e: e.tensor_scalar(
                stcol(hx0, NCH), stcol(hx0, NCH), 0.0, None, ALU.max), inc=dve)
            v1 = dve.n
            kb.wait("scalar", dve, v1)
            kb.op("scalar", lambda e: e.activation(
                stcol(xsq0, NCH), stcol(hx0, NCH), AF.Square), inc=act)
            a1 = act.n
            kb.wait("vector", act, a1)
            kb.op("vector", lambda e: e.tensor_scalar(
                stcol(m2c0, NCH), stcol(xsq0, NCH), stcol(s1sq), EPS,
                ALU.mult, ALU.max), inc=dve)
            gap("vector")
            kb.op("vector", lambda e: e.reciprocal(
                stcol(rtc0, NCH), stcol(m2c0, NCH)), inc=dve)
            gap("vector")
            kb.op("vector", lambda e: e.tensor_scalar(
                stcol(sc20, NCH), stcol(m2c0, NCH), stcol(s2c), 1.0 / 255.0,
                ALU.mult, ALU.mult), inc=dve)
            kb.op("vector", lambda e: e.tensor_scalar(
                stcol(rtc0, NCH), stcol(rtc0, NCH), stcol(s1sq), 255.0,
                ALU.mult, ALU.mult), inc=dve)
            v2 = dve.n
            kb.wait("scalar", dve, v2)
            kb.op("scalar", lambda e: e.activation(
                stcol(rtc0, NCH), stcol(rtc0, NCH), AF.Sqrt), inc=act)
            coef_ready = act.n

            if "ffnst" in dbg:
                dffs = outp("dbg_ffnst", [P, 160])
                kb.wait("sync", act, coef_ready)
                kb.wait("sync", dve, dve.n)
                kb.dma("sync", lambda e, d=dffs: e.dma_start(out=d, in_=ST[:]),
                       inc=sto)

            kb.wait("sync", act, coef_ready)
            q2_ready = {}
            ldq1_ev = {}
            mmB_ev = {}
            xup_ev = {}
            quantB_done = [0]
            for g in range(8):
                for sp in range(2):
                    if g >= 1:
                        kb.wait("sync", pe, mmB_ev[(g - 1, NCH - 1)])
                    kb.dma("sync", lambda e, s_=q1d[2 * g + sp], d=Q1S[sp]:
                        e.dma_start(out=d, in_=s_), inc=sd)
                    ldq1_ev[(g, sp)] = sd.n
                for sp in range(2):
                    s2_ = 2 * g + sp
                    for j in range(4):
                        jpar = j % 2
                        gi = (g * 8 + sp * 4 + j)
                        if gi >= 2:
                            kb.wait("sync", dve, quantB_done[0] if False else q2_ready.get(gi - 2, 0))
                        kb.dma("sync", lambda e, s_=cpj_t[4 * s2_ + j],
                            d=WST[jpar]: e.dma_start(out=d, in_=s_), inc=sd)
                        kb.wait("vector", sd, sd.n)
                        if g >= 1 and j == 0:
                            kb.wait("vector", pe, mmB_ev[(g - 1, NCH - 1)])
                        q2_ready[gi] = quant2048(
                            jpar, rs2c, Q2S[sp][:, j * 2048:(j + 1) * 2048])
                for tc in range(NCH):
                    kb.wait("tensor", sd, ldq1_ev[(g, 1)])
                    kb.wait("tensor", dve, q2_ready[g * 8 + 7])
                    if (g, tc - 1) in mmB_ev:
                        pass
                    for sp in range(2):
                        if ("relu", g, tc - 1, sp) in mmB_ev:
                            kb.wait("tensor", act, mmB_ev[("relu", g, tc - 1, sp)])
                        elif g >= 1 and ("relu", g - 1, NCH - 1, sp) in mmB_ev:
                            kb.wait("tensor", act, mmB_ev[("relu", g - 1, NCH - 1, sp)])
                        for dc in range(NFC):
                            kb.op("tensor", lambda e, dc=dc, tc=tc, sp=sp:
                                e.matmul(PSF2[:, sp * 512:(sp + 1) * 512],
                                         XNT[:, dc * 1024 + tc * 128:dc * 1024 + (tc + 1) * 128],
                                         Q1S[sp][:, dc * 512:(dc + 1) * 512],
                                         start=(dc == 0), stop=(dc == NFC - 1)),
                                inc=pe if dc == NFC - 1 else None)
                    htil = pe.n
                    kb.wait("scalar", pe, htil)
                    for sp in range(2):
                        kb.op("scalar", lambda e, sp=sp, tc=tc: e.activation(
                            T1[sp], PSF2[:, sp * 512:(sp + 1) * 512], AF.Relu,
                            scale=stcol(rtc0 + tc)), inc=act)
                        mmB_ev[("relu", g, tc, sp)] = act.n
                    for sp in range(2):
                        if aic_ev[sp]:
                            kb.wait("scalar", dve, aic_ev[sp])
                        kb.op("scalar", lambda e, sp=sp: e.activation(
                            AI32[:, sp * 512:(sp + 1) * 512], T1[sp], AF.Square),
                            inc=act)
                    sq2 = act.n
                    kb.wait("vector", act, sq2)
                    if ("tr", g, tc - 1) in mmB_ev:
                        kb.wait("vector", pe, mmB_ev[("tr", g, tc - 1)])
                    elif g >= 1 and ("tr", g - 1, NCH - 1) in mmB_ev:
                        kb.wait("vector", pe, mmB_ev[("tr", g - 1, NCH - 1)])
                    kb.op("vector", lambda e: e.tensor_copy(RB16, AI32[:]),
                          inc=dve)
                    aic_ev[0] = dve.n
                    aic_ev[1] = dve.n
                    rbc = dve.n
                    kb.wait("tensor", dve, rbc)
                    lk_acquire("PSB")
                    for j8 in range(8):
                        kb.op("tensor", lambda e, j8=j8:
                            e.transpose(PSB[:, j8 * 128:(j8 + 1) * 128],
                                        RB16[:, j8 * 128:(j8 + 1) * 128], IDB),
                            inc=pe if j8 == 7 else None)
                    mmB_ev[("tr", g, tc)] = pe.n
                    kb.wait("vector", pe, pe.n)
                    kb.op("vector", lambda e: e.tensor_copy(RT16, PSB[:, 0:1024]),
                          inc=dve)
                    rtev = dve.n
                    lk_release("PSB", dve, rtev)
                    kb.wait("tensor", dve, rtev)
                    if (g, tc - 1) in xup_ev:
                        kb.wait("tensor", dve, xup_ev[(g, tc - 1)])
                    elif g >= 1 and (g - 1, NCH - 1) in xup_ev:
                        kb.wait("tensor", dve, xup_ev[(g - 1, NCH - 1)])
                    for kk in range(8):
                        sp, j = kk // 4, kk % 4
                        for nh in range(4):
                            kb.op("tensor", lambda e, kk=kk, sp=sp, j=j, nh=nh:
                                e.matmul(PSF[:, nh * 512:(nh + 1) * 512],
                                         RT16[:, kk * 128:(kk + 1) * 128],
                                         Q2S[sp][:, j * 2048 + nh * 512:j * 2048 + (nh + 1) * 512],
                                         start=(kk == 0), stop=(kk == 7)),
                                inc=pe if (kk == 7 and nh == 3) else None)
                    mmB_ev[(g, tc)] = pe.n
                    kb.wait("vector", pe, mmB_ev[(g, tc)])
                    for hf in range(2):
                        kb.op("vector", lambda e, tc=tc, hf=hf: e.tensor_scalar(
                            TMPX, PSF[:, hf * 1024:(hf + 1) * 1024],
                            stcol(sc20 + tc), None, ALU.mult), inc=dve)
                        kb.op("vector", lambda e, tc=tc, hf=hf: e.tensor_add(
                            X[:, tc * D + hf * 1024:tc * D + (hf + 1) * 1024],
                            X[:, tc * D + hf * 1024:tc * D + (hf + 1) * 1024],
                            TMPX), inc=dve)
                    xup_ev[(g, tc)] = dve.n
            return dict(done_dve=dve.n, done_act=act.n, done_pe=pe.n,
                        done_pool=pool.n)

        ph3 = l0_ffn()

        ln1 = ln_phase(1, after=ph3)
        gwT2 = inp("gwT2", [P, 256])
        CWB = AB16[:, 36864:36992]
        GWB = AB16[:, 36992:37248]

        def gate_phase():
            gt_gate = [0]
            GWF = AF32[:, 6144:6400]
            kb.dma("sync", lambda e: e.dma_start(out=GWF, in_=gwT2), inc=sd)
            gld = sd.n
            kb.wait("vector", sd, gld)
            kb.wait("vector", pe, ln1["done_pe"])
            kb.op("vector", lambda e: e.tensor_copy(GWB, GWF), inc=dve)
            gwb_ev = dve.n
            if "gwb" in dbg:
                dgw = outp("dbg_gwb", [P, 256], BF16)
                dgwf = outp("dbg_gwf", [P, 256])
                kb.wait("sync", dve, gwb_ev)
                kb.dma("sync", lambda e, d=dgw: e.dma_start(out=d, in_=GWB), inc=sto)
                kb.dma("sync", lambda e, d=dgwf: e.dma_start(out=d, in_=GWF), inc=sto)
            kb.wait("tensor", dve, gwb_ev)
            kb.wait("tensor", dve, ln1["done_dve"])
            ev_sm = {}
            ev_smp = {}
            for tc in range(NCH):
                if tc >= 1:
                    kb.wait("tensor", dve, ev_sm[tc - 1])
                    kb.wait("tensor", pool, ev_smp[tc - 1])
                for dc in range(NFC):
                    kb.op("tensor", lambda e, dc=dc, tc=tc:
                        e.matmul(PSF[:, 0:16],
                                 XNT[:, dc * 1024 + tc * 128:dc * 1024 + (tc + 1) * 128],
                                 GWB[:, dc * 16:(dc + 1) * 16],
                                 start=(dc == 0), stop=(dc == NFC - 1)),
                        inc=pe if dc == NFC - 1 else None)
                lg = pe.n
                kb.wait("vector", pe, lg)
                if tc == 0 and "gt" in dbg:
                    kb.op("vector", lambda e: e.tensor_copy(
                        AF32[:, 6400:6416], PSF[:, 0:16]), inc=dve)
                kb.op("vector", lambda e: e.reduce_max(
                    ST[:, 155:156], PSF[:, 0:16], axis=AX.X), inc=dve)
                gap("vector")
                kb.op("vector", lambda e: e.tensor_scalar(
                    ST[:, 152:153], ST[:, 155:156], -1.0, None, ALU.mult), inc=dve)
                nm = dve.n
                kb.wait("scalar", dve, nm)
                kb.op("scalar", lambda e: e.activation(
                    ST[:, 136:152], PSF[:, 0:16], AF.Exp, bias=ST[:, 152:153],
                    accum_out=ST[:, 153:154]), inc=act)
                ex = act.n
                kb.wait("vector", act, ex)
                kb.op("vector", lambda e: e.max(
                    out=ST[:, 96:104], in_=ST[:, 136:152]), inc=dve)
                kb.op("vector", lambda e: e.reciprocal(
                    ST[:, 154:155], ST[:, 153:154]), inc=dve)
                gap("vector")
                kb.op("vector", lambda e: e.tensor_single_scalar(
                    ST[:, 104:120], ST[:, 136:152], ST[:, 101:102], ALU.is_ge),
                    inc=dve)
                ige = dve.n
                kb.wait("gpsimd", dve, ige)
                kb.op("gpsimd", lambda e: e.tensor_mul(
                    AF32[:, 6440:6456], ST[:, 136:152], ST[:, 104:120]),
                    inc=pool)
                pm = pool.n
                kb.wait("vector", pool, pm)
                kb.op("vector", lambda e, tc=tc: e.tensor_scalar(
                    CWB[:, tc * 16:(tc + 1) * 16], AF32[:, 6440:6456],
                    ST[:, 154:155], None, ALU.mult), inc=dve)
                ev_sm[tc] = dve.n
                ev_smp[tc] = pool.n
                if tc == 0 and "gt" in dbg:
                    kb.op("vector", lambda e: e.tensor_copy(
                        AF32[:, 6416:6440], ST[:, 132:156]), inc=dve)
                    dgt = outp("dbg_gt", [P, 160])
                    dgt2 = outp("dbg_gt2", [P, 64])
                    kb.wait("sync", dve, dve.n)
                    kb.dma("sync", lambda e, d=dgt: e.dma_start(out=d, in_=ST[:]),
                           inc=sto)
                    kb.dma("sync", lambda e, d=dgt2: e.dma_start(
                        out=d, in_=AF32[:, 6400:6464]), inc=sto)
                    gt_gate[0] = sto.n
                if tc == 1 and gt_gate[0]:
                    kb.wait("tensor", sto, gt_gate[0])
            return dict(done_dve=dve.n, done_act=act.n, done_pe=pe.n)

        ph4 = gate_phase()

        if "xnt1" in dbg:
            dxnt1 = outp("dbg_xnt1", [P, NFC * TOK], BF16)
            kb.wait("sync", dve, ln1["done_dve"])
            kb.dma("sync", lambda e, d=dxnt1: e.dma_start(out=d, in_=XNT[:]),
                   inc=sto)

        wqT = inp("L1_wqT", [D, D])
        wkT = inp("L1_wkT", [D, D])
        wvT = inp("L1_wvT", [D, D])
        woT = inp("L1_woT", [D, D])
        bl1 = inp("L1_bias", [P, 64])
        ropeC = inp("ropeC", [16, P, TOK])
        ropeS = inp("ropeS", [16, P, TOK])
        iotaf = inp("iotaf", [P, D])
        thr_in = inp("thr", [P, NCH])
        kv1_in = nc.dram_tensor("kv1_in", [2 * D, TOK], BF16, kind="Internal").ap()
        kv1_out = nc.dram_tensor("kv1_out", [2, 2 * D, TOK], BF16, kind="Internal").ap()

        ATT1 = AB16[:, 0:16384]
        WPC1 = AB16[:, 16384:18432]
        MASKQ = AB16[:, 18432:20480]
        KVS1 = [AB16[:, 20480:21504], AB16[:, 21504:22528]]
        QTH1 = [AB16[:, 22528:23552], AB16[:, 23552:24576]]
        KTH1 = AB16[:, 24576:26624]
        VTR1 = AB16[:, 26624:28672]
        VTL1 = AB16[:, 28672:30720]
        PBr1 = AB16[:, 30720:32768]
        PBX1 = [AB16[:, 32768:34816], VTR1]
        PTG1 = AB16[:, 34816:36864]
        CSTG1 = AF32[:, 0:2048]
        RPC = AF32[:, 2048:3072]
        RPS = AF32[:, 3072:4096]
        IOTAF = AF32[:, 4096:6144]
        THR = CONS[:, 256:264]

        def l1_attn():
            for eng in ("sync", "scalar", "vector", "tensor", "gpsimd"):
                kb.wait(eng, dve, ph4["done_dve"])
                kb.wait(eng, act, ph4["done_act"])
                kb.wait(eng, pe, ph4["done_pe"])
                kb.wait(eng, pool, pool.n)
            kb.dma("sync", lambda e: e.dma_start(out=BIAS[:, 0:64], in_=bl1), inc=sd)
            kb.dma("sync", lambda e: e.dma_start(out=IOTAF, in_=iotaf), inc=sd)
            kb.dma("sync", lambda e: e.dma_start(out=CONS[:, 256:264], in_=thr_in),
                   inc=sd)
            lb1 = sd.n
            kb.wait("vector", sd, lb1)
            kb.op("vector", lambda e: e.tensor_scalar(
                BIAS[:, 64:80], BIAS[:, 0:16], 0.125, None, ALU.mult), inc=dve)
            bqs1 = dve.n

            wmap = {"q": wqT, "k": wkT, "v": wvT, "o": woT}
            wp_ready = {}
            wp_done = {}
            pidx = [0]
            cst_free = [(None, 0)]

            def prep_plain(kind, pc):
                i = pidx[0]
                pidx[0] += 1
                if i >= 1:
                    s_, v_ = cst_free[0]
                    if s_ is not None:
                        kb.wait("sync", s_, v_)
                    if wp_done.get(i - 1) is not None:
                        kb.wait("vector", pe, wp_done[i - 1])
                src = wmap[kind].rearrange("(dc p) f -> p dc f", p=P)[:, :, pc * 128:(pc + 1) * 128]
                dst = CSTG1.rearrange("p (dc f) -> p dc f", f=128)
                kb.dma("sync", lambda e, s=src, d=dst: e.dma_start(out=d, in_=s),
                       inc=sd)
                ldn = sd.n
                kb.wait("vector", sd, ldn)
                kb.op("vector", lambda e: e.tensor_copy(WPC1, CSTG1), inc=dve)
                wp_ready[i] = dve.n
                cst_free[0] = (dve, dve.n)
                return i

            def proj_mms(i, psf_half, rhs):
                kb.wait("tensor", dve, wp_ready[i])
                lname = "PSF0" if psf_half == 0 else "PSF1"
                lk_acquire(lname)
                dstp = PSF[:, psf_half * 1024:(psf_half + 1) * 1024]
                for dc in range(NFC):
                    for nh in range(2):
                        kb.op("tensor", lambda e, dc=dc, nh=nh,
                            d=dstp[:, nh * 512:(nh + 1) * 512], r=rhs(dc, nh):
                            e.matmul(d, WPC1[:, dc * 128:(dc + 1) * 128], r,
                                     start=(dc == 0), stop=(dc == NFC - 1)),
                            inc=pe if (dc == NFC - 1 and nh == 1) else None)
                wp_done[i] = pe.n
                return dstp, lname

            def xnt_rhs(dc, nh):
                return XNT[:, dc * 1024 + nh * 512:dc * 1024 + (nh + 1) * 512]

            def att_rhs(dc, nh):
                return ATT1[:, dc * 1024 + nh * 512:dc * 1024 + (nh + 1) * 512]

            rope_tab_prev = [0]

            def rope_apply(dstp, bias_col, scale, raw, t1_dst, t2_dst, out_dst,
                           free_guard, raw_guard=None, pc=0):
                if rope_tab_prev[0]:
                    kb.wait("sync", dve, rope_tab_prev[0])
                kb.dma("sync", lambda e, s=ropeC[pc]: e.dma_start(out=RPC, in_=s),
                       inc=sd)
                kb.dma("sync", lambda e, s=ropeS[pc]: e.dma_start(out=RPS, in_=s),
                       inc=sd)
                tab_ld = sd.n
                kb.wait("scalar", pe, pe.n)
                if raw_guard is not None and raw_guard[0] is not None:
                    kb.wait("scalar", raw_guard[0], raw_guard[1])
                kb.op("scalar", lambda e: e.activation(
                    raw, dstp, AF.Identity, bias=bias_col, scale=scale), inc=act)
                rev = act.n
                kb.wait("tensor", act, rev)
                lk_acquire("PSF2")
                for nh in range(2):
                    kb.op("tensor", lambda e, nh=nh: e.matmul(
                        PSF2[:, nh * 512:(nh + 1) * 512], PERM,
                        raw[:, nh * 512:(nh + 1) * 512],
                        start=True, stop=True), inc=pe if nh == 1 else None)
                pm = pe.n
                kb.wait("vector", pe, pm)
                kb.wait("vector", sd, tab_ld)
                if free_guard is not None and free_guard[0] is not None:
                    kb.wait("vector", free_guard[0], free_guard[1])
                kb.op("vector", lambda e: e.tensor_mul(t1_dst, raw, RPC), inc=dve)
                kb.op("vector", lambda e: e.tensor_mul(t2_dst, PSF2[:], RPS),
                      inc=dve)
                kb.op("vector", lambda e: e.tensor_add(out_dst, t1_dst, t2_dst),
                      inc=dve)
                rope_tab_prev[0] = dve.n
                lk_release("PSF2", dve, dve.n)
                return dve.n

            kb.wait("scalar", sd, lb1)
            kv1_st = [(None, 0), (None, 0)]
            rope_prev = [0]
            for kind, row0, bcol, rope in (("k", 0, 16, True), ("v", D, 32, False)):
                for pc in range(NFC):
                    i = prep_plain(kind, pc)
                    sp = i % 2
                    dstp, lname = proj_mms(i, i % 2, xnt_rhs)
                    s_, v_ = kv1_st[sp]
                    if rope:
                        ev = rope_apply(dstp, BIAS[:, bcol + pc:bcol + pc + 1],
                                        1.0, PBr1[:, 0:1024], KVS1[sp],
                                        PBr1[:, 1024:2048], KVS1[sp],
                                        (s_, v_) if s_ is not None else None,
                                        raw_guard=(dve, rope_prev[0]) if rope_prev[0] else None,
                                        pc=pc)
                        rope_prev[0] = ev
                        lk_release(lname, act, act.n)
                    else:
                        kb.wait("scalar", pe, pe.n)
                        if s_ is not None:
                            kb.wait("scalar", s_, v_)
                        kb.op("scalar", lambda e, d=KVS1[sp], s2=dstp,
                            b=BIAS[:, bcol + pc:bcol + pc + 1]:
                            e.activation(d, s2, AF.Identity, bias=b), inc=act)
                        ev = act.n
                        lk_release(lname, act, ev)
                    eng_sem = dve if rope else act
                    kb.wait("gpsimd", eng_sem, ev)
                    kb.dma("gpsimd", lambda e,
                        d=kv1_in[row0 + pc * 128:row0 + (pc + 1) * 128, :],
                        s=KVS1[sp]: e.dma_start(out=d, in_=s), inc=gd)
                    kv1_st[sp] = (gd, gd.n)
            cc1 = kb.sem("cc1")
            kb.wait("gpsimd", gd, gd.n)
            kb.op("gpsimd", lambda e: e.collective_compute(
                "AllGather", ALU.bypass,
                replica_groups=[[0, 1], [2, 3], [4, 5], [6, 7]],
                ins=[kv1_in], outs=[kv1_out]), inc=cc1)

            ev_exp = {}
            ev_stt = {}
            ev_ptev = {}
            ev_comb = {}
            ev_qth = {}
            sc_last = {}
            vtr_pe = {}
            ev_vtl = {}
            av_pe_last = [0]
            pend = []

            def emit_tr1(hp16, sub, qc):
                pbuf = PBX1[qc % 2]
                kb.wait("tensor", pool, ev_ptev[("norm", hp16, sub, qc)])
                lk_acquire("PSB")
                for kc in range(16):
                    kb.op("tensor", lambda e, kc=kc, pbuf=pbuf:
                        e.transpose(PSB[:, kc * 128:(kc + 1) * 128],
                                    pbuf[:, kc * 128:(kc + 1) * 128], IDB),
                        inc=pe if kc == 15 else None)
                trp = pe.n
                kb.wait("vector", pe, trp)
                if av_pe_last[0]:
                    kb.wait("vector", pe, av_pe_last[0])
                kb.op("vector", lambda e: e.tensor_copy(PTG1, PSB[:]), inc=dve)
                ev_ptev[(hp16, sub, qc)] = (dve.n, trp)
                lk_release("PSB", dve, dve.n)
                kb.wait("tensor", dve, ev_ptev[(hp16, sub, qc)][0])
                kb.wait("tensor", dve, ev_vtl[hp16])
                lk_acquire("PSF2")
                for kc in range(16):
                    kb.op("tensor", lambda e, kc=kc, sub=sub: e.matmul(
                        PSF2[0:64, 0:128],
                        VTL1[:, kc * 128 + sub * 64:kc * 128 + (sub + 1) * 64],
                        PTG1[:, kc * 128:(kc + 1) * 128],
                        start=(kc == 0), stop=(kc == 15)),
                        inc=pe if kc == 15 else None)
                av = pe.n
                av_pe_last[0] = av
                kb.wait("vector", pe, av)
                kb.op("vector", lambda e, fc=hp16, hh=sub, qc=qc: e.tensor_copy(
                    ATT1[hh * 64:(hh + 1) * 64,
                         fc * 1024 + qc * 128:fc * 1024 + (qc + 1) * 128],
                    PSF2[0:64, 0:128]), inc=dve)
                lk_release("PSF2", dve, dve.n)

            for hp16 in range(16):
                par = hp16 % 2
                i = prep_plain("q", hp16)
                dstp, lname = proj_mms(i, i % 2, xnt_rhs)
                fg = (pe, sc_last[hp16 - 2]) if hp16 >= 2 else None
                if hp16 == 0:
                    rg = kv1_st[0]
                elif hp16 == 1:
                    rg = kv1_st[1]
                else:
                    rg = (dve, ev_qth[hp16 - 1])
                rope_apply(dstp, BIAS[:, 64 + hp16:65 + hp16], 0.125,
                           KVS1[0], KVS1[1], PBr1[:, 0:1024], QTH1[par], fg,
                           raw_guard=rg, pc=hp16)
                lk_release(lname, act, act.n)
                ev_qth[hp16] = dve.n

                if hp16 == 0:
                    kb.wait("scalar", cc1, 1)
                else:
                    kb.wait("scalar", pe, sc_last[hp16 - 1])
                    kb.wait("scalar", pe, ev_ptev[(hp16 - 1, 1, NCH - 1)][1])
                src = kv1_out[:, hp16 * 128:(hp16 + 1) * 128, :].rearrange("r p t -> p r t")
                kb.dma("scalar", lambda e, s=src: e.dma_start(
                    out=KTH1[:].rearrange("p (r t) -> p r t", r=2), in_=s), inc=ad)
                srcv = kv1_out[:, D + hp16 * 128:D + (hp16 + 1) * 128, :].rearrange("r p t -> p r t")
                kb.dma("scalar", lambda e, s=srcv: e.dma_start(
                    out=VTR1[:].rearrange("p (r t) -> p r t", r=2), in_=s), inc=ad)
                ld_kv = ad.n

                kb.wait("tensor", ad, ld_kv)
                lk_acquire("PSB")
                for kc in range(16):
                    kb.op("tensor", lambda e, kc=kc:
                        e.transpose(PSB[:, kc * 128:(kc + 1) * 128],
                                    VTR1[:, kc * 128:(kc + 1) * 128], IDB),
                        inc=pe if kc == 15 else None)
                vtr_pe[hp16] = pe.n
                kb.wait("vector", pe, pe.n)
                if av_pe_last[0]:
                    kb.wait("vector", pe, av_pe_last[0])
                kb.op("vector", lambda e: e.tensor_copy(VTL1, PSB[:]), inc=dve)
                ev_vtl[hp16] = dve.n
                lk_release("PSB", dve, dve.n)

                for sub in range(2):
                    for qc in range(NCH):
                        kb.op("vector", lambda e, qc=qc: e.tensor_scalar(
                            MASKQ, IOTAF, THR[:, qc:qc + 1], None, ALU.is_le),
                            inc=dve)
                        mk = dve.n
                        kb.wait("tensor", dve, ev_qth[hp16])
                        if sub == 0 and qc == 0:
                            kb.wait("tensor", ad, ld_kv)
                        for b2 in range(4):
                            prev = (hp16, sub, qc - 1, b2)
                            if qc == 0:
                                prev = (hp16, sub - 1, NCH - 1, b2) if sub else \
                                       (hp16 - 1, 1, NCH - 1, b2)
                            if prev in ev_exp:
                                kb.wait("tensor", act, ev_exp[prev])
                            kb.op("tensor", lambda e, b2=b2, par=par, sub=sub,
                                qc=qc: e.matmul(
                                PSF[:, b2 * 512:(b2 + 1) * 512],
                                QTH1[par][sub * 64:(sub + 1) * 64, qc * 128:(qc + 1) * 128],
                                KTH1[sub * 64:(sub + 1) * 64, b2 * 512:(b2 + 1) * 512],
                                start=True, stop=True), inc=pe)
                            mmb = pe.n
                            kb.wait("scalar", pe, mmb)
                            prevt = (hp16, sub, qc - 2)
                            if qc < 2:
                                prevt = (hp16, sub - 1, NCH - 2 + qc) if sub else \
                                        (hp16 - 1, 1, NCH - 2 + qc)
                            if prevt in ev_ptev and b2 == 0:
                                kb.wait("scalar", pe, ev_ptev[prevt][1])
                            prevc = (hp16, sub, qc - 1)
                            if qc == 0:
                                prevc = (hp16, sub - 1, NCH - 1) if sub else \
                                        (hp16 - 1, 1, NCH - 1)
                            if prevc in ev_stt and b2 == 0:
                                kb.wait("scalar", dve, ev_stt[prevc])
                            kb.op("scalar", lambda e, b2=b2: e.activation(
                                PBr1[:, b2 * 512:(b2 + 1) * 512],
                                PSF[:, b2 * 512:(b2 + 1) * 512], AF.Exp), inc=act)
                            ev_exp[(hp16, sub, qc, b2)] = act.n
                        sc_last[hp16] = pe.n
                        kb.wait("vector", act, ev_exp[(hp16, sub, qc, 3)])
                        if prevt in ev_ptev:
                            kb.wait("vector", pe, ev_ptev[prevt][1])
                        if sub == 0 and qc == 1:
                            kb.wait("vector", pe, vtr_pe[hp16])
                        kb.op("vector", lambda e, qc=qc: e.scalar_tensor_tensor(
                            out=PBX1[qc % 2], in0=PBr1, scalar=1.0, in1=MASKQ,
                            op0=ALU.bypass, op1=ALU.mult,
                            accum_out=ST[:, 100:101]), inc=dve)
                        ev_stt[(hp16, sub, qc)] = dve.n
                        gap("vector")
                        jj = (hp16 * 2 + sub) * NCH + qc
                        rpar = jj % 2
                        if ("rcol", rpar) in ev_ptev:
                            kb.wait("vector", pool, ev_ptev[("rcol", rpar)])
                        kb.op("vector", lambda e, rp=rpar: e.reciprocal(
                            ST[:, 101 + rp:102 + rp], ST[:, 100:101]), inc=dve)
                        rc = dve.n
                        kb.wait("gpsimd", dve, rc)
                        kb.op("gpsimd", lambda e, rp=rpar, pbuf=PBX1[qc % 2]:
                            e.tensor_scalar(pbuf[:], pbuf[:],
                                            ST[:, 101 + rp:102 + rp], None,
                                            ALU.mult), inc=pool)
                        ev_ptev[("norm", hp16, sub, qc)] = pool.n
                        ev_ptev[("rcol", rpar)] = pool.n
                        pend.append((hp16, sub, qc))
                        if len(pend) >= 2:
                            emit_tr1(*pend.pop(0))
                    while pend:
                        emit_tr1(*pend.pop(0))

            last_exp = ev_exp[(15, 1, NCH - 1, 3)]
            lk_release("PSF0", act, last_exp)
            lk_release("PSF1", act, last_exp)
            kvo_free = [(None, 0), (None, 0)]
            for pc in range(NFC):
                i = prep_plain("o", pc)
                sp = i % 2
                dstp, lname = proj_mms(i, i % 2, att_rhs)
                kb.wait("scalar", pe, pe.n)
                fs, fv = kvo_free[sp]
                if fs is not None:
                    kb.wait("scalar", fs, fv)
                kb.op("scalar", lambda e, d=KVS1[sp], s2=dstp,
                    b=BIAS[:, 48 + pc:48 + pc + 1]:
                    e.activation(d, s2, AF.Identity, bias=b), inc=act)
                ev = act.n
                lk_release(lname, act, ev)
                kb.wait("tensor", act, ev)
                lk_acquire("PSB")
                for tc in range(NCH):
                    kb.op("tensor", lambda e, tc=tc, sp=sp:
                        e.transpose(PSB[:, tc * 128:(tc + 1) * 128],
                                    KVS1[sp][:, tc * 128:(tc + 1) * 128], IDB),
                        inc=pe if tc == NCH - 1 else None)
                tro = pe.n
                kvo_free[sp] = (pe, tro)
                kb.wait("vector", pe, tro)
                dstx = X[:].rearrange("p (c f) -> p c f", f=D)[:, :, pc * 128:(pc + 1) * 128]
                srcx = PSB[:, 0:1024].rearrange("p (c f) -> p c f", f=128)
                kb.op("vector", lambda e, d=dstx, s=srcx:
                    e.tensor_add(d, d, s), inc=dve)
                lk_release("PSB", dve, dve.n)
            return dict(done_dve=dve.n, done_act=act.n, done_pe=pe.n,
                        done_pool=pool.n)

        ph5 = l1_attn()

        if "att1" in dbg:
            datt1 = outp("dbg_att1", [P, 16 * TOK], BF16)
            kb.wait("sync", dve, ph5["done_dve"])
            kb.dma("sync", lambda e, d=datt1: e.dma_start(out=d, in_=ATT1), inc=sto)

        wg_t = inp("wg_t", [16, 11, P, D])
        wu_t = inp("wu_t", [16, 11, P, D])
        wd_t = inp("wd_t", [16, 11, P, D])
        MOE_M = 11

        ZB = AB16[:, 0:11264]
        WD = AB16[:, 11264:33792]
        WGP = [AB16[:, 33792:35840], AB16[:, 35840:37888]]
        CWB2 = AF32[:, 6144:6272]
        CSTG2 = AF32[:, 0:2048]
        GS = AF32[:, 2048:3072]
        WDST = AF32[:, 3072:5120]
        TMPX2 = AF32[:, 5120:6144]

        def moe():
            for eng in ("sync", "scalar", "vector", "tensor", "gpsimd"):
                kb.wait(eng, dve, ph5["done_dve"])
                kb.wait(eng, act, ph5["done_act"])
                kb.wait(eng, pe, ph5["done_pe"])
                kb.wait(eng, pool, pool.n)
            kb.op("vector", lambda e: e.tensor_copy(CWB2, CWB), inc=dve)
            cw_moved = dve.n
            kb.wait("sync", dve, cw_moved)
            wg_ready = {}
            wg_done = {}
            wd_ready = {}
            ev_z = {}
            ev_silu = {}
            ev_down = {}
            ev_xup = {}
            stg_free = {}
            wd_cast_read = [0]

            for ex in range(16):
                for mc in range(MOE_M):
                    gi = ex * MOE_M + mc
                    if gi >= 1:
                        kb.wait("gpsimd", pool, wd_ready.get(gi - 1, 0))
                    if ex >= 1 and mc == 0:
                        kb.wait("gpsimd", pe, ev_down[ex - 1])
                    kb.dma("gpsimd", lambda e, s=wd_t[ex, mc]:
                        e.dma_start(out=WDST, in_=s), inc=gd)
                    kb.wait("gpsimd", gd, gd.n)
                    kb.op("gpsimd", lambda e, mc=mc: e.tensor_copy(
                        WD[:, mc * 2048:(mc + 1) * 2048], WDST), inc=pool)
                    wd_ready[gi] = pool.n

                for mc in range(MOE_M):
                    for kind, wt in (("g", wg_t), ("u", wu_t)):
                        gi2 = (ex * MOE_M + mc) * 2 + (0 if kind == "g" else 1)
                        par = gi2 % 2
                        if gi2 >= 1:
                            kb.wait("sync", dve, wg_ready[gi2 - 1])
                        kb.dma("sync", lambda e, s=wt[ex, mc]:
                            e.dma_start(out=CSTG2, in_=s), inc=sd)
                        ldn = sd.n
                        kb.wait("vector", sd, ldn)
                        if gi2 >= 2:
                            kb.wait("vector", pe, wg_done.get(gi2 - 2, 0))
                        kb.op("vector", lambda e, par=par: e.tensor_copy(
                            WGP[par], CSTG2), inc=dve)
                        wg_ready[gi2] = dve.n
                        kb.wait("tensor", dve, wg_ready[gi2])
                        if kind == "g":
                            lk_acquire("PSF2")
                            dstp = PSF2[:]
                        else:
                            lk_acquire("PSF0")
                            dstp = PSF[:, 0:1024]
                        for dc in range(NFC):
                            for nh in range(2):
                                kb.op("tensor", lambda e, dc=dc, nh=nh, par=par,
                                    d=dstp[:, nh * 512:(nh + 1) * 512]:
                                    e.matmul(d, WGP[par][:, dc * 128:(dc + 1) * 128],
                                             XNT[:, dc * 1024 + nh * 512:dc * 1024 + (nh + 1) * 512],
                                             start=(dc == 0), stop=(dc == NFC - 1)),
                                    inc=pe if (dc == NFC - 1 and nh == 1) else None)
                        wg_done[gi2] = pe.n
                        if kind == "g":
                            kb.wait("scalar", pe, pe.n)
                            prev_z = ev_z.get((ex, mc - 1)) if mc >= 1 else \
                                (ev_z.get((ex - 1, MOE_M - 1)) if ex >= 1 else None)
                            if prev_z:
                                kb.wait("scalar", dve, prev_z)
                            kb.op("scalar", lambda e: e.activation(
                                GS, PSF2[:], AF.Silu), inc=act)
                            ev_silu[(ex, mc)] = act.n
                            lk_release("PSF2", act, act.n)
                        else:
                            kb.wait("vector", pe, pe.n)
                            kb.wait("vector", act, ev_silu[(ex, mc)])
                            if ex >= 1 and mc == 0:
                                kb.wait("vector", pe, ev_down[ex - 1])
                            kb.op("vector", lambda e, mc=mc: e.tensor_mul(
                                ZB[:, mc * 1024:(mc + 1) * 1024], GS,
                                PSF[:, 0:1024]), inc=dve)
                            ev_z[(ex, mc)] = dve.n
                            lk_release("PSF0", dve, dve.n)

                kb.wait("tensor", dve, ev_z[(ex, MOE_M - 1)])
                kb.wait("tensor", pool, wd_ready[ex * MOE_M + MOE_M - 1])
                lk_acquire("PSF0")
                lk_acquire("PSF1")
                for tc in range(NCH):
                    if (ex, tc - 1) in ev_xup:
                        kb.wait("tensor", dve, ev_xup[(ex, tc - 1)])
                    elif ex >= 1 and (ex - 1, NCH - 1) in ev_xup:
                        kb.wait("tensor", dve, ev_xup[(ex - 1, NCH - 1)])
                    for mc in range(MOE_M):
                        for nh in range(4):
                            kb.op("tensor", lambda e, mc=mc, nh=nh, tc=tc:
                                e.matmul(PSF[:, nh * 512:(nh + 1) * 512],
                                         ZB[:, mc * 1024 + tc * 128:mc * 1024 + (tc + 1) * 128],
                                         WD[:, mc * 2048 + nh * 512:mc * 2048 + (nh + 1) * 512],
                                         start=(mc == 0), stop=(mc == MOE_M - 1)),
                                inc=pe if (mc == MOE_M - 1 and nh == 3) else None)
                    dn = pe.n
                    kb.wait("vector", pe, dn)
                    for hf in range(2):
                        kb.op("vector", lambda e, tc=tc, hf=hf, ex=ex:
                            e.tensor_scalar(
                            TMPX2, PSF[:, hf * 1024:(hf + 1) * 1024],
                            CWB2[:, tc * 16 + ex:tc * 16 + ex + 1], None,
                            ALU.mult), inc=dve)
                        kb.op("vector", lambda e, tc=tc, hf=hf: e.tensor_add(
                            X[:, tc * D + hf * 1024:tc * D + (hf + 1) * 1024],
                            X[:, tc * D + hf * 1024:tc * D + (hf + 1) * 1024],
                            TMPX2), inc=dve)
                    ev_xup[(ex, tc)] = dve.n
                ev_down[ex] = pe.n
                lk_release("PSF0", dve, dve.n)
                lk_release("PSF1", dve, dve.n)
            return dict(done_dve=dve.n, done_act=act.n, done_pe=pe.n,
                        done_pool=pool.n)

        ph6 = moe()

        if "cw" in dbg:
            dcw = outp("dbg_cw", [P, 128], BF16)
            kb.wait("sync", dve, ph4["done_dve"])
            kb.dma("sync", lambda e, d=dcw: e.dma_start(out=d, in_=CWB), inc=sto)



        if "st" in dbg:
            dst_ = outp("dbg_st", [P, 160], F32)
            kb.wait("sync", dve, dve.n)
            kb.wait("sync", act, act.n)
            kb.dma("sync", lambda e: e.dma_start(out=dst_, in_=ST[:]), inc=sto)
        if "xnt" in dbg:
            dxnt = outp("dbg_xnt", [P, NFC * TOK], BF16)
            kb.wait("sync", dve, ln0["done_dve"])
            kb.dma("sync", lambda e: e.dma_start(out=dxnt, in_=XNT[:]), inc=sto)

        kb.wait("sync", dve, dve.n)
        kb.wait("sync", act, act.n)
        kb.wait("sync", pe, pe.n)
        if pool.n:
            kb.wait("sync", pool, pool.n)
        kb.dma("sync", lambda e: e.dma_start(
            out=out_x.rearrange("(c p) f -> p c f", p=P),
            in_=X[:].rearrange("p (c f) -> p c f", f=D)), inc=sto)
        kb.wait("sync", sto, sto.n)

        with nc.Block() as block:
            kb.finalize(block)

    return nc, io


def _consts_host():
    cf = np.zeros((P, 640), np.float32)
    cf[:, 0:128] = np.eye(P, dtype=np.float32)
    cf[:, 320:448] = 1.0
    cf[:, 576] = EPS
    cb = np.zeros((P, 512), np.float32)
    cb[:, 0:128] = np.eye(P, dtype=np.float32)
    perm = np.zeros((P, P), np.float32)
    sw = np.concatenate([np.arange(32, 64), np.arange(0, 32),
                         np.arange(96, 128), np.arange(64, 96)])
    perm[sw, np.arange(P)] = 1.0
    cb[:, 128:256] = perm.T
    cb[:, 256:384] = 1.0
    for b in range(4):
        cb[b, 384 + b * 32:384 + (b + 1) * 32] = 0.0
    jj = np.arange(128)
    for b in range(4):
        cb[b, 384:512] = (jj // 32 == b).astype(np.float32)
    return cf, cb.astype(ml_dtypes.bfloat16)


def _scale_pieces(scales):
    s = np.asarray(scales, np.float32).reshape(16, 4, 16, 128)
    return np.ascontiguousarray(s.transpose(2, 1, 0, 3).reshape(16, 4, 2048))


def _bias_cols(bias):
    return np.ascontiguousarray(np.asarray(bias, np.float32).reshape(16, P).T)


def make_in_maps(inputs):
    x = np.ascontiguousarray(np.asarray(inputs["L0_attn_x"], np.float32))
    cf, cb = _consts_host()
    f32c = lambda a: np.ascontiguousarray(np.asarray(a, np.float32))
    shared = {
        "consts": cf,
        "constb": cb,
        "L0q_codes": f32c(inputs["L0_attn_wq_codes"]),
        "L0k_codes": f32c(inputs["L0_attn_wk_codes"]),
        "L0v_codes": f32c(inputs["L0_attn_wv_codes"]),
        "L0o_codes": f32c(inputs["L0_attn_wo_codes"]),
        "L0q_scp": _scale_pieces(inputs["L0_attn_wq_scales"]),
        "L0k_scp": _scale_pieces(inputs["L0_attn_wk_scales"]),
        "L0v_scp": _scale_pieces(inputs["L0_attn_wv_scales"]),
        "L0o_scp": _scale_pieces(inputs["L0_attn_wo_scales"]),
        "L0_bias": np.concatenate(
            [_bias_cols(inputs["L0_attn_" + k]) for k in ("bq", "bk", "bv", "bo")],
            axis=1),
        "cfc_t": np.ascontiguousarray(
            np.asarray(inputs["L0_ffn_c_fc"], np.float32)
            .reshape(D, 16, 512).transpose(1, 0, 2)),
        "cpj_t": np.ascontiguousarray(
            np.asarray(inputs["L0_ffn_c_proj"], np.float32)
            .T.reshape(64, P, D)),
        "gwT2": np.ascontiguousarray(
            np.asarray(inputs["L1_ffn_gate_w"], np.float32)
            .reshape(16, 16, P).transpose(2, 1, 0).reshape(P, 256)),
        "L1_wqT": np.ascontiguousarray(np.asarray(inputs["L1_attn_wq_w"], np.float32).T),
        "L1_wkT": np.ascontiguousarray(np.asarray(inputs["L1_attn_wk_w"], np.float32).T),
        "L1_wvT": np.ascontiguousarray(np.asarray(inputs["L1_attn_wv_w"], np.float32).T),
        "L1_woT": np.ascontiguousarray(np.asarray(inputs["L1_attn_wo_w"], np.float32).T),
        "L1_bias": np.concatenate(
            [_bias_cols(inputs["L1_attn_" + k]) for k in ("wq_b", "wk_b", "wv_b", "wo_b")],
            axis=1),
        "iotaf": np.ascontiguousarray(
            np.tile(np.arange(D, dtype=np.float32)[None, :], (P, 1))),
        "wg_t": np.ascontiguousarray(
            np.asarray(inputs["L1_ffn_w_gate"], np.float32)
            .reshape(16, 11, P, 16, P).transpose(0, 1, 4, 3, 2)
            .reshape(16, 11, P, D)),
        "wu_t": np.ascontiguousarray(
            np.asarray(inputs["L1_ffn_w_up"], np.float32)
            .reshape(16, 11, P, 16, P).transpose(0, 1, 4, 3, 2)
            .reshape(16, 11, P, D)),
        "wd_t": np.ascontiguousarray(
            np.asarray(inputs["L1_ffn_w_down"], np.float32)
            .transpose(0, 2, 1).reshape(16, 11, P, D)),
    }
    tsv = 10000.0 ** (np.arange(32, dtype=np.float64) / 32.0)

    def _rope_tabs(h):
        pos = np.arange(TOK, dtype=np.float64) + h * TOK
        C = np.zeros((16, P, TOK), np.float32)
        S = np.zeros((16, P, TOK), np.float32)
        for fc in range(16):
            for sub in range(2):
                hd = 2 * fc + sub
                ang = pos / tsv[hd]
                c = np.cos(ang).astype(np.float32)
                s = np.sin(ang).astype(np.float32)
                r0 = sub * 64
                C[fc, r0:r0 + 64] = c[None, :]
                S[fc, r0:r0 + 32] = -s[None, :]
                S[fc, r0 + 32:r0 + 64] = s[None, :]
        return np.ascontiguousarray(C), np.ascontiguousarray(S)
    maps = []
    for c in range(8):
        b, h = c // 2, c % 2
        m = dict(shared)
        m["xs"] = np.ascontiguousarray(x[b, h * TOK:(h + 1) * TOK, :])
        Cc, Ss = _rope_tabs(h)
        m["ropeC"] = Cc
        m["ropeS"] = Ss
        thr = (h * TOK + np.arange(NCH)[None, :] * P
               + np.arange(P)[:, None]).astype(np.float32)
        m["thr"] = np.ascontiguousarray(thr)
        maps.append(m)
    return maps


def kernel(**inputs):
    nc, _ = build_core()
    maps = make_in_maps(inputs)
    res = run_bass_kernel_spmd(nc, maps, core_ids=list(range(8)))
    out = np.zeros((B, S, D), np.float32)
    for c in range(8):
        b, h = c // 2, c % 2
        out[b, h * TOK:(h + 1) * TOK, :] = res.results[c]["out_x"]
    return out


# version 08 (final)
# speedup vs baseline: 1.5870x; 1.5870x over previous
import numpy as np
import ml_dtypes
from contextlib import ExitStack

import concourse.bass as bass
import concourse.mybir as mybir
from concourse.bass_utils import run_bass_kernel_spmd

F32 = mybir.dt.float32
BF16 = mybir.dt.bfloat16
I32 = mybir.dt.int32
AF = mybir.ActivationFunctionType
ALU = mybir.AluOpType
AX = mybir.AxisListType

P = 128
D = 2048
TOK = 1024
NCH = TOK // P
NFC = D // P
B, S = 4, 2048
H_FFN = 8192
EPS = 1e-5

DT_BYTES = {F32: 4, BF16: 2, I32: 4}


class Sem:
    def __init__(self, kb, name):
        self.h = kb.ctx.enter_context(kb.nc.semaphore(name))
        self.n = 0

    def bump(self, k=1):
        self.n += k
        return self.n


class Kb:
    ENGS = ("sync", "scalar", "vector", "tensor", "gpsimd")

    def __init__(self, nc, ctx):
        self.nc = nc
        self.ctx = ctx
        self.ops = {e: [] for e in self.ENGS}
        self.dbg = {}

    def sem(self, name):
        return Sem(self, name)

    def emit(self, eng, fn):
        self.ops[eng].append(fn)

    def wait(self, eng, sem, val):
        assert val <= sem.n, f"wait on {val} but only {sem.n} scheduled"
        self.emit(eng, lambda e: e.wait_ge(sem.h, val))

    def op(self, eng, build, inc=None, k=1):
        if inc is not None:
            inc.bump(k)
        def f(e):
            ins = build(e)
            if inc is not None:
                ins.then_inc(inc.h, k)
        self.emit(eng, f)
        return inc.n if inc is not None else None

    def dma(self, eng, build, inc):
        return self.op(eng, build, inc=inc, k=16)

    def finalize(self, block):
        for eng in self.ENGS:
            thunks = self.ops[eng]
            getattr(block, eng)(lambda e, th=thunks: [f(e) for f in th])


def build_core(dbg=()):
    dbg = set(dbg)
    nc = bass.Bass("TRN2", target_bir_lowering=False)

    io = {}

    def inp(name, shape, dtype=F32):
        io[name] = nc.dram_tensor(name, list(shape), dtype, kind="ExternalInput").ap()
        return io[name]

    def outp(name, shape, dtype=F32):
        io[name] = nc.dram_tensor(name, list(shape), dtype, kind="ExternalOutput").ap()
        return io[name]

    xs = inp("xs", [TOK, D])
    consts = inp("consts", [P, 640])
    constb = inp("constb", [P, 512], BF16)
    out_x = outp("out_x", [TOK, D])

    ctx = ExitStack()
    with ctx:
        kb = Kb(nc, ctx)
        e = ctx.enter_context

        X = e(nc.sbuf_tensor("X", [P, NCH * D], F32))
        XNT = e(nc.sbuf_tensor("XNT", [P, NFC * TOK], BF16))
        ST = e(nc.sbuf_tensor("ST", [P, 160], F32))
        CONS = e(nc.sbuf_tensor("CONS", [P, 640], F32))
        CONB = e(nc.sbuf_tensor("CONB", [P, 512], BF16))
        AF32 = e(nc.sbuf_tensor("AF32", [P, 6944], F32))
        AB16 = e(nc.sbuf_tensor("AB16", [P, 38912], BF16))

        PSF = e(nc.psum_tensor("PSF", [P, 2048], F32))
        PSB = e(nc.psum_tensor("PSB", [P, 2048], BF16))
        PSF2 = e(nc.psum_tensor("PSF2", [P, 1024], F32))

        sd = kb.sem("sd")
        ad = kb.sem("ad")
        gd = kb.sem("gd")
        pe = kb.sem("pe")
        act = kb.sem("act")
        dve = kb.sem("dve")
        pool = kb.sem("pool")
        sto = kb.sem("sto")

        kb.dma("sync", lambda e: e.dma_start(
            out=X[:].rearrange("p (c f) -> p c f", f=D),
            in_=xs.rearrange("(c p) f -> p c f", p=P)), inc=sd)
        kb.dma("sync", lambda e: e.dma_start(out=CONS[:], in_=consts), inc=sd)
        kb.dma("sync", lambda e: e.dma_start(out=CONB[:], in_=constb), inc=sd)
        ld_done = sd.n

        ID32 = CONS[:, 0:128]
        EPSC = CONS[:, 576:577]
        IDB = CONB[:, 0:128]
        PERM = CONB[:, 128:256]
        ONEB = CONB[:, 256:257]

        def ln_phase(L, after=None):
            if after is not None:
                for eng in ("sync", "scalar", "vector", "tensor", "gpsimd"):
                    kb.wait(eng, dve, after["done_dve"])
                    kb.wait(eng, act, after["done_act"])
                    kb.wait(eng, pe, after["done_pe"])
                    if pool.n:
                        kb.wait(eng, pool, pool.n)
            c_sum, c_ssq, c_m, c_var, c_sd, c_rstd, c_nm, c_m2 = 0, 8, 16, 24, 32, 40, 48, 56
            SQ = AF32[:, 0:2048]
            CENT0 = AF32[:, 2048:4096]
            CENT1 = AF32[:, 4096:6144]
            CENT = [CENT0, CENT1]
            XNTM0 = AB16[:, 0:2048]
            XNTM1 = AB16[:, 2048:4096]
            XNTM = [XNTM0, XNTM1]

            kb.wait("vector", sd, ld_done)
            kb.wait("scalar", sd, ld_done)

            a_sums = []
            a_ssq = []
            for c in range(NCH):
                Xc = X[:, c * D:(c + 1) * D]
                a_sums.append(kb.op("vector", lambda e, Xc=Xc, c=c: e.reduce_sum(
                    ST[:, c_sum + c:c_sum + c + 1], Xc, axis=AX.X), inc=dve))
                a_ssq.append(kb.op("scalar", lambda e, Xc=Xc, c=c: e.activation(
                    SQ, Xc, AF.Square,
                    accum_out=ST[:, c_ssq + c:c_ssq + c + 1]), inc=act))

            def stc(col, c):
                return ST[:, col + c:col + c + 1]

            c_t = 64
            kb.wait("vector", act, a_ssq[-1])
            kb.op("vector", lambda e: e.tensor_scalar(
                ST[:, c_m:c_m + NCH], ST[:, c_sum:c_sum + NCH],
                1.0 / D, None, ALU.mult), inc=dve)
            kb.op("vector", lambda e: e.tensor_scalar(
                ST[:, c_nm:c_nm + NCH], ST[:, c_sum:c_sum + NCH],
                -1.0 / D, None, ALU.mult), inc=dve)
            kb.op("vector", lambda e: e.tensor_scalar(
                ST[:, c_t:c_t + NCH], ST[:, c_ssq:c_ssq + NCH],
                1.0 / D, None, ALU.mult), inc=dve)
            b1 = dve.n
            kb.wait("scalar", dve, b1)
            kb.op("scalar", lambda e: e.activation(
                ST[:, c_m2:c_m2 + NCH], ST[:, c_m:c_m + NCH], AF.Square), inc=act)
            b2 = act.n
            kb.wait("vector", act, b2)
            kb.op("vector", lambda e: e.tensor_sub(
                ST[:, c_var:c_var + NCH], ST[:, c_t:c_t + NCH],
                ST[:, c_m2:c_m2 + NCH]), inc=dve)
            b3 = dve.n
            kb.wait("scalar", dve, b3)
            kb.op("scalar", lambda e: e.activation(
                ST[:, c_sd:c_sd + NCH], ST[:, c_var:c_var + NCH],
                AF.Sqrt, bias=EPSC), inc=act)
            b4 = act.n
            kb.wait("vector", act, b4)
            kb.op("vector", lambda e: e.reciprocal(
                ST[:, c_rstd:c_rstd + NCH], ST[:, c_sd:c_sd + NCH]), inc=dve)
            b_rstd = dve.n

            ev_cent = {}
            ev_xn = {}
            ev_tr = {}
            ev_evac = {}
            dbg_gate = {}
            kb.wait("scalar", dve, b_rstd)
            for c in range(NCH):
                par = c % 2
                Xc = X[:, c * D:(c + 1) * D]
                nm = stc(c_nm, c)
                rstd = stc(c_rstd, c)
                if c >= 2:
                    kb.wait("scalar", dve, ev_xn[c - 2])
                if c == 2 and dbg_gate:
                    kb.wait("vector", sto, dbg_gate["sto"])
                ev_cent[c] = kb.op("scalar", lambda e, Xc=Xc, nm=nm, d=CENT[par]:
                    e.activation(d, Xc, AF.Identity, bias=nm), inc=act)
                kb.wait("vector", act, ev_cent[c])
                if c >= 2:
                    kb.wait("vector", pe, ev_tr[c - 2])
                ev_xn[c] = kb.op("vector", lambda e, d=XNTM[par], s=CENT[par],
                    rstd=rstd: e.tensor_scalar(d, s, rstd, None, ALU.mult),
                    inc=dve)
                if c == 0 and L == 0 and "xn0" in dbg:
                    dxn0 = outp("dbg_xn0", [P, D], BF16)
                    kb.wait("sync", dve, ev_xn[0])
                    kb.dma("sync", lambda e, d=dxn0: e.dma_start(out=d, in_=XNTM0),
                           inc=sto)
                    dbg_gate["sto"] = sto.n
                kb.wait("tensor", dve, ev_xn[c])
                if c >= 1:
                    kb.wait("tensor", dve, ev_evac[c - 1])
                for fc in range(NFC):
                    last = fc == NFC - 1
                    kb.op("tensor", lambda e, d=PSB[:, fc * 128:(fc + 1) * 128],
                        s=XNTM[par][:, fc * 128:(fc + 1) * 128]:
                        e.transpose(d, s, IDB),
                        inc=pe if last else None)
                ev_tr[c] = pe.n
                kb.wait("vector", pe, ev_tr[c])
                dst = XNT[:].rearrange("p (f t) -> p f t", f=NFC)[:, :, c * 128:(c + 1) * 128]
                src = PSB[:].rearrange("p (f t) -> p f t", f=NFC)
                ev_evac[c] = kb.op("vector", lambda e, dst=dst, src=src:
                    e.tensor_copy(dst, src), inc=dve)
            return dict(done_pe=pe.n, done_dve=dve.n, done_act=act.n)

        ln0 = ln_phase(0)

        NH0 = 16
        cq = inp("L0q_codes", [D, D])
        ck = inp("L0k_codes", [D, D])
        cv = inp("L0v_codes", [D, D])
        co = inp("L0o_codes", [D, D])
        sq = inp("L0q_scp", [16, 4, D])
        sk = inp("L0k_scp", [16, 4, D])
        sv = inp("L0v_scp", [16, 4, D])
        so = inp("L0o_scp", [16, 4, D])
        bl0 = inp("L0_bias", [P, 64])

        kv_in = nc.dram_tensor("kv_in", [2 * D, TOK], BF16, kind="Internal").ap()
        kv_out = nc.dram_tensor("kv_out", [2, 2 * D, TOK], BF16, kind="Internal").ap()

        ATT = AB16[:, 0:16384]
        WPC = [AB16[:, 16384:18432], AB16[:, 18432:20480]]
        KVST = [AB16[:, 20480:21504], AB16[:, 21504:22528]]
        RAW0 = AB16[:, 20480:22528]
        QTH = [AB16[:, 22528:23552], AB16[:, 23552:24576]]
        KTH = AB16[:, 24576:26624]
        VTR = AB16[:, 26624:28672]
        VTL = AB16[:, 28672:30720]
        PB = AB16[:, 30720:32768]
        PTG = AB16[:, 32768:36864]
        SCB = AB16[0:4, 36864:38912]
        CSTG = [AF32[:, 0:2048], AF32[:, 2048:4096]]
        SCP = AF32[0:4, 4096:6144]
        E4B = CONB[0:4, 384:512]
        GAPC = ST[:, 158:159]

        def gap(eng):
            kb.op(eng, lambda e: e.tensor_scalar(GAPC, GAPC, 1.0, None, ALU.mult))

        def barrier():
            for eng in ("tensor", "scalar", "vector", "gpsimd", "sync"):
                for s in (pe, act, dve, pool, sd, ad, gd):
                    if s.n:
                        kb.wait(eng, s, s.n)

        locks = {"PSF2": None, "PSB": None, "PSF0": None, "PSF1": None}

        def lk_acquire(name):
            if locks[name] is not None:
                s, v = locks[name]
                kb.wait("tensor", s, v)

        def lk_release(name, s, v):
            locks[name] = (s, v)

        BIAS = e(nc.sbuf_tensor("BIAS", [P, 80], F32))
        kb.dma("sync", lambda e: e.dma_start(out=BIAS[:, 0:64], in_=bl0), inc=sd)
        bias_ld = sd.n
        kb.wait("vector", sd, bias_ld)
        kb.op("vector", lambda e: e.tensor_scalar(
            BIAS[:, 64:80], BIAS[:, 0:16], float(1.0 / np.sqrt(128.0)),
            None, ALU.mult), inc=dve)

        def l0_attn():
            kb.wait("sync", act, ln0["done_act"])
            kb.wait("sync", dve, ln0["done_dve"])
            scv = {"q": sq, "k": sk, "v": sv, "o": so}
            cov = {"q": cq, "k": ck, "v": cv, "o": co}
            wp_ready = {}
            wp_done = {}
            wp_exp_pe = {}
            kvst_free = [(None, 0), (None, 0)]
            pidx = [0]
            scb_cast = [0]

            def prep_piece(kind, pc):
                i = pidx[0]
                pidx[0] += 1
                par = i % 2
                if i >= 2:
                    kb.wait("sync", dve, wp_ready[i - 2])
                if i >= 1:
                    kb.wait("sync", dve, scb_cast[0])
                src = cov[kind].rearrange("(dc p) f -> p dc f", p=P)[:, :, pc * 128:(pc + 1) * 128]
                dst = CSTG[par].rearrange("p (dc f) -> p dc f", f=128)
                kb.dma("sync", lambda e, s=src, d=dst: e.dma_start(out=d, in_=s), inc=sd)
                kb.dma("sync", lambda e, s=scv[kind][pc], d=SCP:
                    e.dma_start(out=d, in_=s), inc=sd)
                ld = sd.n
                kb.wait("vector", sd, ld)
                if i >= 1:
                    kb.wait("vector", pe, wp_exp_pe[i - 1])
                kb.op("vector", lambda e: e.tensor_copy(SCB, SCP), inc=dve)
                scb_cast[0] = dve.n
                scb_ready = dve.n
                kb.wait("tensor", dve, scb_ready)
                if i >= 2 and wp_done.get(i - 2) is not None:
                    pass
                for rnd in range(2):
                    lk_acquire("PSF2")
                    for j in range(8):
                        dc = rnd * 8 + j
                        kb.op("tensor", lambda e, j=j, dc=dc, par=par:
                            e.matmul(PSF2[:, j * 128:(j + 1) * 128], E4B,
                                     SCB[:, dc * 128:(dc + 1) * 128],
                                     start=True, stop=True),
                            inc=pe if j == 7 else None)
                    tr = pe.n
                    kb.wait("vector", pe, tr)
                    if rnd == 0 and i >= 2 and wp_done.get(i - 2) is not None:
                        kb.wait("vector", pe, wp_done[i - 2])
                    kb.op("vector", lambda e, rnd=rnd, par=par:
                        e.tensor_mul(WPC[par][:, rnd * 1024:(rnd + 1) * 1024],
                                     CSTG[par][:, rnd * 1024:(rnd + 1) * 1024],
                                     PSF2[:]), inc=dve)
                    lk_release("PSF2", dve, dve.n)
                wp_exp_pe[i] = pe.n
                wp_ready[i] = dve.n
                return i, par

            def piece_matmuls(i, par, rhs, psf_half, bias_col, stage, sp,
                              scale=1.0):
                lname = "PSF0" if psf_half == 0 else "PSF1"
                kb.wait("tensor", dve, wp_ready[i])
                lk_acquire(lname)
                dstp = PSF[:, psf_half * 1024:(psf_half + 1) * 1024]
                for dc in range(NFC):
                    for nh in range(2):
                        kb.op("tensor", lambda e, dc=dc, nh=nh, par=par,
                            d=dstp[:, nh * 512:(nh + 1) * 512], r=rhs(dc, nh):
                            e.matmul(d, WPC[par][:, dc * 128:(dc + 1) * 128], r,
                                     start=(dc == 0), stop=(dc == NFC - 1)),
                            inc=pe if (dc == NFC - 1 and nh == 1) else None)
                wp_done[i] = pe.n
                mm = pe.n
                kb.wait("scalar", pe, mm)
                fs, fv = kvst_free[sp]
                if fs is not None:
                    kb.wait("scalar", fs, fv)
                kb.op("scalar", lambda e, d=stage, s=dstp, b=bias_col:
                    e.activation(d, s, AF.Identity, bias=b, scale=scale), inc=act)
                ev = act.n
                lk_release(lname, act, ev)
                return ev

            def xnt_rhs(dc, nh):
                return XNT[:, dc * 1024 + nh * 512:dc * 1024 + (nh + 1) * 512]

            def att_rhs(dc, nh):
                return ATT[:, dc * 1024 + nh * 512:dc * 1024 + (nh + 1) * 512]

            kb.wait("scalar", sd, bias_ld)
            prod_ev = {0: None, 1: None}
            for kind, row0, bcol in (("k", 0, 16), ("v", D, 32)):
                for pc in range(NFC):
                    i, par = prep_piece(kind, pc)
                    sp = i % 2
                    ev = piece_matmuls(i, par, xnt_rhs, i % 2,
                                       BIAS[:, bcol + pc:bcol + pc + 1], KVST[sp], sp)
                    prod_ev[i % 2] = ev
                    kb.wait("gpsimd", act, ev)
                    kb.dma("gpsimd", lambda e,
                        d=kv_in[row0 + pc * 128:row0 + (pc + 1) * 128, :],
                        s=KVST[sp]: e.dma_start(out=d, in_=s), inc=gd)
                    kvst_free[sp] = (gd, gd.n)
            cc = kb.sem("cc0")
            kb.wait("gpsimd", gd, gd.n)
            kb.op("gpsimd", lambda e: e.collective_compute(
                "AllGather", ALU.bypass,
                replica_groups=[[0, 1], [2, 3], [4, 5], [6, 7]],
                ins=[kv_in], outs=[kv_out]), inc=cc)

            if "kvout" in dbg:
                dkv = outp("dbg_kvout", [2, 2 * D, TOK], BF16)
                kb.wait("sync", cc, 1)
                kb.dma("sync", lambda e, d=dkv: e.dma_start(out=d, in_=kv_out),
                       inc=sto)

            ev_exp = {}
            ev_ptev = {}
            ev_qth = {}
            sc_last = {}
            vtr_pe = {}
            pend_tr = []
            ev_vtl = {}
            av_pe_last = [0]
            qth_dbg_gate = [0]
            pb_dbg_gate = [0]
            ev_comb = {}
            PBX = [PB, VTR]
            for b in range(4):
                ev_exp[(-1, NCH - 1, b)] = prod_ev[0] if b < 2 else prod_ev[1]
            kb.wait("scalar", gd, gd.n)

            def emit_transposes(h, qc):
                hp = h % 2
                pbuf = PBX[qc % 2]
                kb.wait("tensor", act, ev_ptev[("norm", h, qc)])
                lk_acquire("PSB")
                for kc in range(16):
                    kb.op("tensor", lambda e, kc=kc, pbuf=pbuf:
                        e.transpose(PSB[:, kc * 128:(kc + 1) * 128],
                                    pbuf[:, kc * 128:(kc + 1) * 128], IDB),
                        inc=pe if kc == 15 else None)
                trp = pe.n
                kb.wait("vector", pe, trp)
                if av_pe_last[0]:
                    kb.wait("vector", pe, av_pe_last[0])
                dstp = PTG.rearrange("p (kc q) -> p kc q", q=256)[:, :, (qc % 2) * 128:(qc % 2) * 128 + 128]
                srcp = PSB[:].rearrange("p (kc q) -> p kc q", q=128)
                kb.op("vector", lambda e, d=dstp, s=srcp:
                    e.tensor_copy(d, s), inc=dve)
                ev_ptev[(h, qc)] = (dve.n, trp)
                lk_release("PSB", dve, dve.n)
                if qc % 2 == 1:
                    g = qc // 2
                    kb.wait("tensor", dve, ev_ptev[(h, qc)][0])
                    kb.wait("tensor", dve, ev_vtl[h])
                    lk_acquire("PSF2")
                    for kc in range(16):
                        kb.op("tensor", lambda e, kc=kc:
                            e.matmul(PSF2[:, 0:256],
                                     VTL[:, kc * 128:(kc + 1) * 128],
                                     PTG[:, kc * 256:(kc + 1) * 256],
                                     start=(kc == 0), stop=(kc == 15)),
                            inc=pe if kc == 15 else None)
                    av = pe.n
                    av_pe_last[0] = av
                    kb.wait("vector", pe, av)
                    kb.op("vector", lambda e, h=h, g=g: e.tensor_copy(
                        ATT[:, h * 1024 + g * 256:h * 1024 + (g + 1) * 256],
                        PSF2[:, 0:256]), inc=dve)
                    lk_release("PSF2", dve, dve.n)
                    if h == 0 and g == 0 and "att00" in dbg:
                        da0 = outp("dbg_att00", [P, 256], BF16)
                        dpt0 = outp("dbg_ptg0", [P, 4096], BF16)
                        dvt0 = outp("dbg_vtl0", [P, 2048], BF16)
                        kb.wait("sync", dve, dve.n)
                        kb.dma("sync", lambda e, d=da0: e.dma_start(
                            out=d, in_=ATT[:, 0:256]), inc=sto)
                        kb.dma("sync", lambda e, d=dpt0: e.dma_start(
                            out=d, in_=PTG), inc=sto)
                        kb.dma("sync", lambda e, d=dvt0: e.dma_start(
                            out=d, in_=VTL), inc=sto)
                        pb_dbg_gate[0] = sto.n

            for h in range(NH0):
                hp = h % 2
                i, par = prep_piece("q", h)
                kb.wait("tensor", dve, wp_ready[i])
                for rnd in range(2):
                    lk_acquire("PSF2")
                    for dc in range(NFC):
                        kb.op("tensor", lambda e, dc=dc, rnd=rnd, par=par:
                            e.matmul(PSF2[:, rnd * 512:(rnd + 1) * 512],
                                     WPC[par][:, dc * 128:(dc + 1) * 128],
                                     XNT[:, dc * 1024 + rnd * 512:dc * 1024 + (rnd + 1) * 512],
                                     start=(dc == 0), stop=(dc == NFC - 1)),
                            inc=pe if dc == NFC - 1 else None)
                    wp_done[i] = pe.n
                    kb.wait("scalar", pe, pe.n)
                    if h >= 2 and rnd == 0:
                        kb.wait("scalar", pe, sc_last[h - 2])
                    kb.op("scalar", lambda e, rnd=rnd, hp=hp, h=h:
                        e.activation(QTH[hp][:, rnd * 512:(rnd + 1) * 512],
                                     PSF2[:, rnd * 512:(rnd + 1) * 512],
                                     AF.Identity, bias=BIAS[:, 64 + h:65 + h],
                                     scale=float(1.0 / np.sqrt(128.0))), inc=act)
                    lk_release("PSF2", act, act.n)
                ev_qth[h] = act.n
                if h == 0 and "qth0" in dbg:
                    dqt = outp("dbg_qth0", [P, TOK], BF16)
                    kb.wait("sync", act, ev_qth[0])
                    kb.dma("sync", lambda e, d=dqt: e.dma_start(out=d, in_=QTH[0]),
                           inc=sto)
                    qth_dbg_gate[0] = sto.n
                if h == 2 and qth_dbg_gate[0]:
                    kb.wait("scalar", sto, qth_dbg_gate[0])

                if h == 0:
                    kb.wait("scalar", cc, 1)
                else:
                    kb.wait("scalar", pe, sc_last[h - 1])
                    kb.wait("scalar", pe, ev_ptev[(h - 1, NCH - 1)][1])
                src = kv_out[:, h * 128:(h + 1) * 128, :].rearrange("r p t -> p r t")
                kb.dma("scalar", lambda e, s=src: e.dma_start(
                    out=KTH[:].rearrange("p (r t) -> p r t", r=2), in_=s), inc=ad)
                srcv = kv_out[:, D + h * 128:D + (h + 1) * 128, :].rearrange("r p t -> p r t")
                kb.dma("scalar", lambda e, s=srcv: e.dma_start(
                    out=VTR[:].rearrange("p (r t) -> p r t", r=2), in_=s), inc=ad)
                ld_kv = ad.n

                kb.wait("tensor", ad, ld_kv)
                lk_acquire("PSB")
                for kc in range(16):
                    kb.op("tensor", lambda e, kc=kc:
                        e.transpose(PSB[:, kc * 128:(kc + 1) * 128],
                                    VTR[:, kc * 128:(kc + 1) * 128], IDB),
                        inc=pe if kc == 15 else None)
                vtr_pe[h] = pe.n
                kb.wait("vector", pe, pe.n)
                if av_pe_last[0]:
                    kb.wait("vector", pe, av_pe_last[0])
                kb.op("vector", lambda e: e.tensor_copy(VTL, PSB[:]), inc=dve)
                ev_vtl[h] = dve.n
                lk_release("PSB", dve, dve.n)

                for qc in range(NCH):
                    if "ser" in dbg:
                        barrier()
                    kb.wait("tensor", act, ev_qth[h])
                    if qc == 0:
                        kb.wait("tensor", ad, ld_kv)
                    if pb_dbg_gate[0] and (h, qc) != (0, 0):
                        kb.wait("scalar", sto, pb_dbg_gate[0])
                        pb_dbg_gate[0] = 0
                    for b in range(4):
                        prev = (h, qc - 1, b) if qc >= 1 else (h - 1, NCH - 1, b)
                        if prev in ev_exp:
                            kb.wait("tensor", act, ev_exp[prev])
                        kb.op("tensor", lambda e, b=b, hp=hp, qc=qc:
                            e.matmul(PSF[:, b * 512:(b + 1) * 512],
                                     QTH[hp][:, qc * 128:(qc + 1) * 128],
                                     KTH[:, b * 512:(b + 1) * 512],
                                     start=True, stop=True), inc=pe)
                        mmb = pe.n
                        kb.wait("scalar", pe, mmb)
                        prevc = (h, qc - 1) if qc >= 1 else (h - 1, NCH - 1)
                        if prevc in ev_comb and b == 0:
                            kb.wait("scalar", dve, ev_comb[prevc])
                        kb.op("scalar", lambda e, b=b:
                            e.activation(RAW0[:, b * 512:(b + 1) * 512],
                                         PSF[:, b * 512:(b + 1) * 512], AF.Exp,
                                         accum_out=ST[:, 96 + b:97 + b]), inc=act)
                        ev_exp[(h, qc, b)] = act.n
                    sc_last[h] = pe.n
                    kb.wait("vector", act, ev_exp[(h, qc, 3)])
                    kb.op("vector", lambda e: e.reduce_sum(
                        ST[:, 100:101], ST[:, 96:100], axis=AX.X), inc=dve)
                    ev_comb[(h, qc)] = dve.n
                    gap("vector")
                    jj = h * NCH + qc
                    rpar = jj % 2
                    if ("rcol", rpar) in ev_ptev:
                        kb.wait("vector", act, ev_ptev[("rcol", rpar)])
                    kb.op("vector", lambda e, rp=rpar: e.reciprocal(
                        ST[:, 101 + rp:102 + rp], ST[:, 100:101]), inc=dve)
                    rc = dve.n
                    kb.wait("scalar", dve, rc)
                    prevt = (h, qc - 2) if qc >= 2 else (h - 1, NCH - 2 + qc)
                    if prevt in ev_ptev:
                        kb.wait("scalar", pe, ev_ptev[prevt][1])
                    if qc == 1:
                        kb.wait("scalar", pe, vtr_pe[h])
                    kb.op("scalar", lambda e, rp=rpar, pbuf=PBX[qc % 2]:
                        e.activation(pbuf[:], RAW0, AF.Identity,
                                     scale=ST[:, 101 + rp:102 + rp]), inc=act)
                    ev_ptev[("norm", h, qc)] = act.n
                    ev_ptev[("rcol", rpar)] = act.n
                    if h == 0 and qc == 0 and "pb0" in dbg:
                        dpb = outp("dbg_pb0", [P, 2048], BF16)
                        dstx2 = outp("dbg_sums0", [P, 8])
                        kb.wait("sync", dve, dve.n)
                        kb.wait("sync", pool, ev_ptev[("norm", 0, 0)])
                        kb.dma("sync", lambda e, d=dpb: e.dma_start(out=d, in_=PB[:]),
                               inc=sto)
                        kb.dma("sync", lambda e, d=dstx2: e.dma_start(
                            out=d, in_=ST[:, 96:104]), inc=sto)
                        pb_dbg_gate[0] = sto.n
                    pend_tr.append((h, qc))
                    if len(pend_tr) >= 2:
                        emit_transposes(*pend_tr.pop(0))
                while pend_tr:
                    emit_transposes(*pend_tr.pop(0))

            last_exp = ev_exp[(NH0 - 1, NCH - 1, 3)]
            lk_release("PSF0", act, last_exp)
            lk_release("PSF1", act, last_exp)
            last_norm = ev_ptev[("norm", NH0 - 1, NCH - 1)]
            kvst_free[0] = (act, last_norm)
            kvst_free[1] = (act, last_norm)
            for pc in range(NFC):
                i, par = prep_piece("o", pc)
                sp = i % 2
                ev = piece_matmuls(i, par, att_rhs, i % 2,
                                   BIAS[:, 48 + pc:48 + pc + 1], KVST[sp], sp)
                kb.wait("tensor", act, ev)
                lk_acquire("PSB")
                for tc in range(NCH):
                    kb.op("tensor", lambda e, tc=tc, sp=sp:
                        e.transpose(PSB[:, tc * 128:(tc + 1) * 128],
                                    KVST[sp][:, tc * 128:(tc + 1) * 128], IDB),
                        inc=pe if tc == NCH - 1 else None)
                tro = pe.n
                kvst_free[sp] = (pe, tro)
                kb.wait("vector", pe, tro)
                dstx = X[:].rearrange("p (c f) -> p c f", f=D)[:, :, pc * 128:(pc + 1) * 128]
                srcx = PSB[:, 0:1024].rearrange("p (c f) -> p c f", f=128)
                kb.op("vector", lambda e, d=dstx, s=srcx:
                    e.tensor_add(d, d, s), inc=dve)
                lk_release("PSB", dve, dve.n)
            return dict(done_dve=dve.n, done_act=act.n, done_pe=pe.n)

        ph2 = l0_attn()

        if "att" in dbg:
            datt = outp("dbg_att", [P, 16 * TOK], BF16)
            kb.wait("sync", dve, ph2["done_dve"])
            kb.dma("sync", lambda e, d=datt: e.dma_start(out=d, in_=ATT), inc=sto)

        cfc_t = inp("cfc_t", [16, D, 512])
        cpj_t = inp("cpj_t", [64, P, D])
        q1d = nc.dram_tensor("q1d", [16, P, 8192], BF16, kind="Internal").ap()
        AI32 = e(nc.sbuf_tensor("AI32", [P, 1024], I32))

        def l0_ffn():
            Q1S = [AB16[:, 0:8192], AB16[:, 8192:16384]]
            Q2S = [AB16[:, 16384:24576], AB16[:, 24576:32768]]
            RB16 = AB16[:, 32768:33792]
            RT16 = AB16[:, 33792:34816]
            WST = [AF32[:, 0:2048], AF32[:, 2048:4096]]
            T1 = [AF32[:, 4096:4608], AF32[:, 4608:5120]]
            TMPX = AF32[:, 5120:6144]
            ONE1 = CONS[:, 320:321]
            ONER = CONS[0:1, 320:448]
            sa1, sa2, tmpa = 102, 103, 104
            s1c, s2c, rs1c, rs2c, s1sq = 106, 107, 108, 109, 119
            hx0 = 110
            rtc0 = 120
            sc20 = 128
            m2c0 = 140
            xsq0 = 148

            def stcol(c0, k=1):
                return ST[:, c0:c0 + k]

            kb.wait("sync", dve, ph2["done_dve"])
            kb.wait("sync", act, ph2["done_act"])
            kb.wait("sync", pe, ph2["done_pe"])
            kb.wait("vector", act, ph2["done_act"])
            kb.wait("vector", pe, ph2["done_pe"])
            kb.wait("scalar", dve, ph2["done_dve"])
            kb.wait("scalar", pe, ph2["done_pe"])
            kb.wait("tensor", dve, ph2["done_dve"])
            kb.wait("tensor", act, ph2["done_act"])
            kb.op("vector", lambda e: e.memset(ST[:, sa1:sa1 + 2], 0.0), inc=dve)
            kb.op("vector", lambda e: e.memset(stcol(hx0, NCH), -1e30), inc=dve)

            ld_ev = {}
            for mi in range(2):
                for p in range(64):
                    par = p % 2
                    gi = mi * 64 + p
                    if gi >= 2:
                        kb.wait("sync", dve, ld_ev[gi - 2])
                    if mi == 0:
                        s0, sub = p // 4, p % 4
                        sap = cfc_t[s0, sub * 512:(sub + 1) * 512, :].rearrange(
                            "(dc q) h -> q dc h", q=P)
                        dap = WST[par].rearrange("q (dc h) -> q dc h", h=512)
                    else:
                        sap = cpj_t[p]
                        dap = WST[par]
                    kb.dma("sync", lambda e, s=sap, d=dap:
                        e.dma_start(out=d, in_=s), inc=sd)
                    ldn = sd.n
                    kb.wait("vector", sd, ldn)
                    kb.op("vector", lambda e, par=par: e.reduce_sum(
                        stcol(tmpa + par), WST[par], axis=AX.X,
                        apply_absolute_value=True), inc=dve)
                    gap("vector")
                    kb.op("vector", lambda e, par=par, acc=sa1 + mi: e.tensor_add(
                        stcol(acc), stcol(acc), stcol(tmpa + par)), inc=dve)
                    ld_ev[gi] = dve.n

            abs_done = dve.n
            kb.wait("tensor", dve, abs_done)
            kb.op("tensor", lambda e: e.matmul(
                PSF2[0:1, 0:1], ST[:, sa1:sa1 + 1], ONE1,
                start=True, stop=True))
            kb.op("tensor", lambda e: e.matmul(
                PSF2[0:1, 1:2], ST[:, sa2:sa2 + 1], ONE1,
                start=True, stop=True), inc=pe)
            t_tot = pe.n
            kb.wait("vector", pe, t_tot)
            kb.op("vector", lambda e: e.tensor_copy(CONS[0:1, 578:580],
                                                    PSF2[0:1, 0:2]), inc=dve)
            tot_ev = dve.n
            kb.wait("tensor", dve, tot_ev)
            kb.op("tensor", lambda e: e.matmul(
                PSF2[:, 0:2], ONER, CONS[0:1, 578:580],
                start=True, stop=True), inc=pe)
            t_bc = pe.n
            kb.wait("vector", pe, t_bc)
            kb.op("vector", lambda e: e.tensor_scalar(
                stcol(s1c, 2), PSF2[:, 0:2], 1.0 / (D * H_FFN), EPS,
                ALU.mult, ALU.max), inc=dve)
            sv = dve.n
            kb.wait("scalar", dve, sv)
            kb.op("scalar", lambda e: e.activation(
                stcol(s1sq), stcol(s1c), AF.Square), inc=act)
            sq_ev = act.n
            kb.wait("vector", act, sq_ev)
            kb.op("vector", lambda e: e.reciprocal(stcol(rs1c), stcol(s1c)),
                  inc=dve)
            gap("vector")
            kb.op("vector", lambda e: e.reciprocal(stcol(rs2c), stcol(s2c)),
                  inc=dve)
            s_ready = dve.n

            aic_ev = [0, 0, 0, 0]

            def quant2048(wst_par, rscol, dst):
                kb.op("vector", lambda e, w=WST[wst_par], rs=stcol(rscol):
                    e.tensor_scalar(w, w, rs, None, ALU.mult), inc=dve)
                mu = dve.n
                kb.wait("scalar", dve, mu)
                for r in range(4):
                    rr = r % 2
                    ai = AI32[:, rr * 512:(rr + 1) * 512]
                    if aic_ev[rr]:
                        kb.wait("scalar", dve, aic_ev[rr])
                    kb.op("scalar", lambda e, ai=ai, w=WST[wst_par], r=r:
                        e.activation(ai, w[:, r * 512:(r + 1) * 512],
                                     AF.Identity), inc=act)
                    an = act.n
                    kb.wait("vector", act, an)
                    kb.op("vector", lambda e, ai=ai, d=dst, r=r:
                        e.tensor_scalar(d[:, r * 512:(r + 1) * 512], ai,
                                        -1.0, 1.0, ALU.max, ALU.min), inc=dve)
                    aic_ev[rr] = dve.n
                return dve.n

            kb.wait("sync", dve, s_ready)
            q1_ready = {}
            mmA_ev = {}
            redA_ev = {}
            q1d_st = {}
            for s in range(16):
                par = s % 2
                for sub in range(4):
                    spar = sub % 2
                    if (s, sub - 2) in q1_ready:
                        kb.wait("sync", dve, q1_ready[(s, sub - 2)])
                    elif s >= 1 and (s - 1, sub + 2) in q1_ready:
                        kb.wait("sync", dve, q1_ready[(s - 1, sub + 2)])
                    sap = cfc_t[s, sub * 512:(sub + 1) * 512, :].rearrange(
                        "(dc q) h -> q dc h", q=P)
                    kb.dma("sync", lambda e, s_=sap,
                        d=WST[spar].rearrange("q (dc h) -> q dc h", h=512):
                        e.dma_start(out=d, in_=s_), inc=sd)
                    kb.wait("vector", sd, sd.n)
                    if sub == 0 and (s - 2) in mmA_ev:
                        kb.wait("vector", pe, mmA_ev[s - 2])
                        kb.wait("vector", gd, q1d_st[s - 2])
                    q1_ready[(s, sub)] = quant2048(
                        spar, rs1c, Q1S[par][:, sub * 2048:(sub + 1) * 2048])
                kb.wait("gpsimd", dve, q1_ready[(s, 3)])
                kb.dma("gpsimd", lambda e, d=q1d[s], s_=Q1S[par]:
                    e.dma_start(out=d, in_=s_), inc=gd)
                q1d_st[s] = gd.n
                for tc in range(NCH):
                    tp = tc % 2
                    kb.wait("tensor", dve, q1_ready[(s, 3)])
                    if (s, tc - 2) in redA_ev:
                        kb.wait("tensor", dve, redA_ev[(s, tc - 2)])
                    elif s >= 1 and (s - 1, tc + 6) in redA_ev:
                        kb.wait("tensor", dve, redA_ev[(s - 1, tc + 6)])
                    for dc in range(NFC):
                        kb.op("tensor", lambda e, dc=dc, tc=tc, tp=tp, par=par:
                            e.matmul(PSF2[:, tp * 512:(tp + 1) * 512],
                                     XNT[:, dc * 1024 + tc * 128:dc * 1024 + (tc + 1) * 128],
                                     Q1S[par][:, dc * 512:(dc + 1) * 512],
                                     start=(dc == 0), stop=(dc == NFC - 1)),
                            inc=pe if dc == NFC - 1 else None)
                    mmA_ev[s] = pe.n
                    kb.wait("vector", pe, pe.n)
                    kb.op("vector", lambda e, tp=tp: e.reduce_max(
                        stcol(tmpa + tp), PSF2[:, tp * 512:(tp + 1) * 512],
                        axis=AX.X), inc=dve)
                    gap("vector")
                    kb.op("vector", lambda e, tp=tp, tc=tc: e.tensor_max(
                        stcol(hx0 + tc), stcol(hx0 + tc), stcol(tmpa + tp)),
                        inc=dve)
                    redA_ev[(s, tc)] = dve.n

            hmax_done = dve.n
            kb.op("vector", lambda e: e.tensor_scalar(
                stcol(hx0, NCH), stcol(hx0, NCH), 0.0, None, ALU.max), inc=dve)
            v1 = dve.n
            kb.wait("scalar", dve, v1)
            kb.op("scalar", lambda e: e.activation(
                stcol(xsq0, NCH), stcol(hx0, NCH), AF.Square), inc=act)
            a1 = act.n
            kb.wait("vector", act, a1)
            kb.op("vector", lambda e: e.tensor_scalar(
                stcol(m2c0, NCH), stcol(xsq0, NCH), stcol(s1sq), EPS,
                ALU.mult, ALU.max), inc=dve)
            gap("vector")
            kb.op("vector", lambda e: e.reciprocal(
                stcol(rtc0, NCH), stcol(m2c0, NCH)), inc=dve)
            gap("vector")
            kb.op("vector", lambda e: e.tensor_scalar(
                stcol(sc20, NCH), stcol(m2c0, NCH), stcol(s2c), 1.0 / 255.0,
                ALU.mult, ALU.mult), inc=dve)
            kb.op("vector", lambda e: e.tensor_scalar(
                stcol(rtc0, NCH), stcol(rtc0, NCH), stcol(s1sq), 255.0,
                ALU.mult, ALU.mult), inc=dve)
            v2 = dve.n
            kb.wait("scalar", dve, v2)
            kb.op("scalar", lambda e: e.activation(
                stcol(rtc0, NCH), stcol(rtc0, NCH), AF.Sqrt), inc=act)
            coef_ready = act.n

            if "ffnst" in dbg:
                dffs = outp("dbg_ffnst", [P, 160])
                kb.wait("sync", act, coef_ready)
                kb.wait("sync", dve, dve.n)
                kb.dma("sync", lambda e, d=dffs: e.dma_start(out=d, in_=ST[:]),
                       inc=sto)

            kb.wait("sync", act, coef_ready)
            q2_ready = {}
            ldq1_ev = {}
            mmB_ev = {}
            xup_ev = {}
            quantB_done = [0]
            for g in range(8):
                for sp in range(2):
                    if g >= 1:
                        kb.wait("sync", pe, mmB_ev[(g - 1, NCH - 1)])
                    kb.dma("sync", lambda e, s_=q1d[2 * g + sp], d=Q1S[sp]:
                        e.dma_start(out=d, in_=s_), inc=sd)
                    ldq1_ev[(g, sp)] = sd.n
                for sp in range(2):
                    s2_ = 2 * g + sp
                    for j in range(4):
                        jpar = j % 2
                        gi = (g * 8 + sp * 4 + j)
                        if gi >= 2:
                            kb.wait("sync", dve, quantB_done[0] if False else q2_ready.get(gi - 2, 0))
                        kb.dma("sync", lambda e, s_=cpj_t[4 * s2_ + j],
                            d=WST[jpar]: e.dma_start(out=d, in_=s_), inc=sd)
                        kb.wait("vector", sd, sd.n)
                        if g >= 1 and j == 0:
                            kb.wait("vector", pe, mmB_ev[(g - 1, NCH - 1)])
                        q2_ready[gi] = quant2048(
                            jpar, rs2c, Q2S[sp][:, j * 2048:(j + 1) * 2048])
                for tc in range(NCH):
                    kb.wait("tensor", sd, ldq1_ev[(g, 1)])
                    kb.wait("tensor", dve, q2_ready[g * 8 + 7])
                    if (g, tc - 1) in mmB_ev:
                        pass
                    for sp in range(2):
                        if ("relu", g, tc - 1, sp) in mmB_ev:
                            kb.wait("tensor", act, mmB_ev[("relu", g, tc - 1, sp)])
                        elif g >= 1 and ("relu", g - 1, NCH - 1, sp) in mmB_ev:
                            kb.wait("tensor", act, mmB_ev[("relu", g - 1, NCH - 1, sp)])
                        for dc in range(NFC):
                            kb.op("tensor", lambda e, dc=dc, tc=tc, sp=sp:
                                e.matmul(PSF2[:, sp * 512:(sp + 1) * 512],
                                         XNT[:, dc * 1024 + tc * 128:dc * 1024 + (tc + 1) * 128],
                                         Q1S[sp][:, dc * 512:(dc + 1) * 512],
                                         start=(dc == 0), stop=(dc == NFC - 1)),
                                inc=pe if dc == NFC - 1 else None)
                    htil = pe.n
                    kb.wait("scalar", pe, htil)
                    for sp in range(2):
                        kb.op("scalar", lambda e, sp=sp, tc=tc: e.activation(
                            T1[sp], PSF2[:, sp * 512:(sp + 1) * 512], AF.Relu,
                            scale=stcol(rtc0 + tc)), inc=act)
                        mmB_ev[("relu", g, tc, sp)] = act.n
                    for sp in range(2):
                        if aic_ev[sp]:
                            kb.wait("scalar", dve, aic_ev[sp])
                        kb.op("scalar", lambda e, sp=sp: e.activation(
                            AI32[:, sp * 512:(sp + 1) * 512], T1[sp], AF.Square),
                            inc=act)
                    sq2 = act.n
                    kb.wait("vector", act, sq2)
                    if ("tr", g, tc - 1) in mmB_ev:
                        kb.wait("vector", pe, mmB_ev[("tr", g, tc - 1)])
                    elif g >= 1 and ("tr", g - 1, NCH - 1) in mmB_ev:
                        kb.wait("vector", pe, mmB_ev[("tr", g - 1, NCH - 1)])
                    kb.op("vector", lambda e: e.tensor_copy(RB16, AI32[:]),
                          inc=dve)
                    aic_ev[0] = dve.n
                    aic_ev[1] = dve.n
                    rbc = dve.n
                    kb.wait("tensor", dve, rbc)
                    lk_acquire("PSB")
                    for j8 in range(8):
                        kb.op("tensor", lambda e, j8=j8:
                            e.transpose(PSB[:, j8 * 128:(j8 + 1) * 128],
                                        RB16[:, j8 * 128:(j8 + 1) * 128], IDB),
                            inc=pe if j8 == 7 else None)
                    mmB_ev[("tr", g, tc)] = pe.n
                    kb.wait("vector", pe, pe.n)
                    kb.op("vector", lambda e: e.tensor_copy(RT16, PSB[:, 0:1024]),
                          inc=dve)
                    rtev = dve.n
                    lk_release("PSB", dve, rtev)
                    kb.wait("tensor", dve, rtev)
                    if (g, tc - 1) in xup_ev:
                        kb.wait("tensor", dve, xup_ev[(g, tc - 1)])
                    elif g >= 1 and (g - 1, NCH - 1) in xup_ev:
                        kb.wait("tensor", dve, xup_ev[(g - 1, NCH - 1)])
                    for kk in range(8):
                        sp, j = kk // 4, kk % 4
                        for nh in range(4):
                            kb.op("tensor", lambda e, kk=kk, sp=sp, j=j, nh=nh:
                                e.matmul(PSF[:, nh * 512:(nh + 1) * 512],
                                         RT16[:, kk * 128:(kk + 1) * 128],
                                         Q2S[sp][:, j * 2048 + nh * 512:j * 2048 + (nh + 1) * 512],
                                         start=(kk == 0), stop=(kk == 7)),
                                inc=pe if (kk == 7 and nh == 3) else None)
                    mmB_ev[(g, tc)] = pe.n
                    kb.wait("vector", pe, mmB_ev[(g, tc)])
                    for hf in range(2):
                        kb.op("vector", lambda e, tc=tc, hf=hf: e.tensor_scalar(
                            TMPX, PSF[:, hf * 1024:(hf + 1) * 1024],
                            stcol(sc20 + tc), None, ALU.mult), inc=dve)
                        kb.op("vector", lambda e, tc=tc, hf=hf: e.tensor_add(
                            X[:, tc * D + hf * 1024:tc * D + (hf + 1) * 1024],
                            X[:, tc * D + hf * 1024:tc * D + (hf + 1) * 1024],
                            TMPX), inc=dve)
                    xup_ev[(g, tc)] = dve.n
            return dict(done_dve=dve.n, done_act=act.n, done_pe=pe.n,
                        done_pool=pool.n)

        ph3 = l0_ffn()

        ln1 = ln_phase(1, after=ph3)
        gwT2 = inp("gwT2", [P, 256])
        CWB = AB16[:, 36864:36992]
        GWB = AB16[:, 36992:37248]

        def gate_phase():
            gt_gate = [0]
            GWF = AF32[:, 6144:6400]
            kb.dma("sync", lambda e: e.dma_start(out=GWF, in_=gwT2), inc=sd)
            gld = sd.n
            kb.wait("vector", sd, gld)
            kb.wait("vector", pe, ln1["done_pe"])
            kb.op("vector", lambda e: e.tensor_copy(GWB, GWF), inc=dve)
            gwb_ev = dve.n
            if "gwb" in dbg:
                dgw = outp("dbg_gwb", [P, 256], BF16)
                dgwf = outp("dbg_gwf", [P, 256])
                kb.wait("sync", dve, gwb_ev)
                kb.dma("sync", lambda e, d=dgw: e.dma_start(out=d, in_=GWB), inc=sto)
                kb.dma("sync", lambda e, d=dgwf: e.dma_start(out=d, in_=GWF), inc=sto)
            kb.wait("tensor", dve, gwb_ev)
            kb.wait("tensor", dve, ln1["done_dve"])
            ev_sm = {}
            ev_smp = {}
            for tc in range(NCH):
                if tc >= 1:
                    kb.wait("tensor", dve, ev_sm[tc - 1])
                    kb.wait("tensor", pool, ev_smp[tc - 1])
                for dc in range(NFC):
                    kb.op("tensor", lambda e, dc=dc, tc=tc:
                        e.matmul(PSF[:, 0:16],
                                 XNT[:, dc * 1024 + tc * 128:dc * 1024 + (tc + 1) * 128],
                                 GWB[:, dc * 16:(dc + 1) * 16],
                                 start=(dc == 0), stop=(dc == NFC - 1)),
                        inc=pe if dc == NFC - 1 else None)
                lg = pe.n
                kb.wait("vector", pe, lg)
                if tc == 0 and "gt" in dbg:
                    kb.op("vector", lambda e: e.tensor_copy(
                        AF32[:, 6400:6416], PSF[:, 0:16]), inc=dve)
                kb.op("vector", lambda e: e.reduce_max(
                    ST[:, 155:156], PSF[:, 0:16], axis=AX.X), inc=dve)
                gap("vector")
                kb.op("vector", lambda e: e.tensor_scalar(
                    ST[:, 152:153], ST[:, 155:156], -1.0, None, ALU.mult), inc=dve)
                nm = dve.n
                kb.wait("scalar", dve, nm)
                kb.op("scalar", lambda e: e.activation(
                    ST[:, 136:152], PSF[:, 0:16], AF.Exp, bias=ST[:, 152:153],
                    accum_out=ST[:, 153:154]), inc=act)
                ex = act.n
                kb.wait("vector", act, ex)
                kb.op("vector", lambda e: e.max(
                    out=ST[:, 96:104], in_=ST[:, 136:152]), inc=dve)
                kb.op("vector", lambda e: e.reciprocal(
                    ST[:, 154:155], ST[:, 153:154]), inc=dve)
                gap("vector")
                kb.op("vector", lambda e: e.tensor_single_scalar(
                    ST[:, 104:120], ST[:, 136:152], ST[:, 101:102], ALU.is_ge),
                    inc=dve)
                ige = dve.n
                kb.wait("gpsimd", dve, ige)
                kb.op("gpsimd", lambda e: e.tensor_mul(
                    AF32[:, 6440:6456], ST[:, 136:152], ST[:, 104:120]),
                    inc=pool)
                pm = pool.n
                kb.wait("vector", pool, pm)
                kb.op("vector", lambda e, tc=tc: e.tensor_scalar(
                    CWB[:, tc * 16:(tc + 1) * 16], AF32[:, 6440:6456],
                    ST[:, 154:155], None, ALU.mult), inc=dve)
                ev_sm[tc] = dve.n
                ev_smp[tc] = pool.n
                if tc == 0 and "gt" in dbg:
                    kb.op("vector", lambda e: e.tensor_copy(
                        AF32[:, 6416:6440], ST[:, 132:156]), inc=dve)
                    dgt = outp("dbg_gt", [P, 160])
                    dgt2 = outp("dbg_gt2", [P, 64])
                    kb.wait("sync", dve, dve.n)
                    kb.dma("sync", lambda e, d=dgt: e.dma_start(out=d, in_=ST[:]),
                           inc=sto)
                    kb.dma("sync", lambda e, d=dgt2: e.dma_start(
                        out=d, in_=AF32[:, 6400:6464]), inc=sto)
                    gt_gate[0] = sto.n
                if tc == 1 and gt_gate[0]:
                    kb.wait("tensor", sto, gt_gate[0])
            kb.wait("vector", pool, pool.n)
            kb.op("vector", lambda e: e.tensor_copy(CWB2, CWB), inc=dve)
            return dict(done_dve=dve.n, done_act=act.n, done_pe=pe.n)

        ph4 = gate_phase()

        if "xnt1" in dbg:
            dxnt1 = outp("dbg_xnt1", [P, NFC * TOK], BF16)
            kb.wait("sync", dve, ln1["done_dve"])
            kb.dma("sync", lambda e, d=dxnt1: e.dma_start(out=d, in_=XNT[:]),
                   inc=sto)

        wqT = inp("L1_wqT", [D, D])
        wkT = inp("L1_wkT", [D, D])
        wvT = inp("L1_wvT", [D, D])
        woT = inp("L1_woT", [D, D])
        bl1 = inp("L1_bias", [P, 64])
        ropeC = inp("ropeC", [16, P, TOK])
        ropeS = inp("ropeS", [16, P, TOK])
        iotaf = inp("iotaf", [P, D])
        thr_in = inp("thr", [P, NCH])
        kv1_in = nc.dram_tensor("kv1_in", [2 * D, TOK], BF16, kind="Internal").ap()
        kv1_out = nc.dram_tensor("kv1_out", [2, 2 * D, TOK], BF16, kind="Internal").ap()

        ATT1 = AB16[:, 0:16384]
        WPC1 = AB16[:, 16384:18432]
        MASKQ = AB16[:, 18432:20480]
        KVS1 = [AB16[:, 20480:21504], AB16[:, 21504:22528]]
        QTH1 = [AB16[:, 22528:23552], AB16[:, 23552:24576]]
        KTH1 = AB16[:, 24576:26624]
        VTR1 = AB16[:, 26624:28672]
        VTL1 = AB16[:, 28672:30720]
        PBr1 = AB16[:, 30720:32768]
        PBX1 = [AB16[:, 32768:34816], VTR1]
        PTG1 = AB16[:, 34816:36864]
        NORMT = AB16[:, 36864:38912]
        CSTG1 = AF32[:, 0:2048]
        RPC = AF32[:, 2048:3072]
        RPS = AF32[:, 3072:4096]
        IOTAF = AF32[:, 4096:6144]
        THR = CONS[:, 256:264]

        def l1_attn():
            for eng in ("sync", "scalar", "vector", "tensor", "gpsimd"):
                kb.wait(eng, dve, ph4["done_dve"])
                kb.wait(eng, act, ph4["done_act"])
                kb.wait(eng, pe, ph4["done_pe"])
                kb.wait(eng, pool, pool.n)
            kb.dma("sync", lambda e: e.dma_start(out=BIAS[:, 0:64], in_=bl1), inc=sd)
            kb.dma("sync", lambda e: e.dma_start(out=IOTAF, in_=iotaf), inc=sd)
            kb.dma("sync", lambda e: e.dma_start(out=CONS[:, 256:264], in_=thr_in),
                   inc=sd)
            lb1 = sd.n
            kb.wait("vector", sd, lb1)
            kb.op("vector", lambda e: e.tensor_scalar(
                BIAS[:, 64:80], BIAS[:, 0:16], 0.125, None, ALU.mult), inc=dve)
            bqs1 = dve.n

            wmap = {"q": wqT, "k": wkT, "v": wvT, "o": woT}
            wp_ready = {}
            wp_done = {}
            pidx = [0]
            cst_free = [(None, 0)]

            def prep_plain(kind, pc):
                i = pidx[0]
                pidx[0] += 1
                if i >= 1:
                    s_, v_ = cst_free[0]
                    if s_ is not None:
                        kb.wait("sync", s_, v_)
                    if wp_done.get(i - 1) is not None:
                        kb.wait("vector", pe, wp_done[i - 1])
                src = wmap[kind].rearrange("(dc p) f -> p dc f", p=P)[:, :, pc * 128:(pc + 1) * 128]
                dst = CSTG1.rearrange("p (dc f) -> p dc f", f=128)
                kb.dma("sync", lambda e, s=src, d=dst: e.dma_start(out=d, in_=s),
                       inc=sd)
                ldn = sd.n
                kb.wait("vector", sd, ldn)
                kb.op("vector", lambda e: e.tensor_copy(WPC1, CSTG1), inc=dve)
                wp_ready[i] = dve.n
                cst_free[0] = (dve, dve.n)
                return i

            def proj_mms(i, psf_half, rhs):
                kb.wait("tensor", dve, wp_ready[i])
                lname = "PSF0" if psf_half == 0 else "PSF1"
                lk_acquire(lname)
                dstp = PSF[:, psf_half * 1024:(psf_half + 1) * 1024]
                for dc in range(NFC):
                    for nh in range(2):
                        kb.op("tensor", lambda e, dc=dc, nh=nh,
                            d=dstp[:, nh * 512:(nh + 1) * 512], r=rhs(dc, nh):
                            e.matmul(d, WPC1[:, dc * 128:(dc + 1) * 128], r,
                                     start=(dc == 0), stop=(dc == NFC - 1)),
                            inc=pe if (dc == NFC - 1 and nh == 1) else None)
                wp_done[i] = pe.n
                return dstp, lname

            def xnt_rhs(dc, nh):
                return XNT[:, dc * 1024 + nh * 512:dc * 1024 + (nh + 1) * 512]

            def att_rhs(dc, nh):
                return ATT1[:, dc * 1024 + nh * 512:dc * 1024 + (nh + 1) * 512]

            rope_tab_prev = [0]

            def rope_apply(dstp, bias_col, scale, raw, t1_dst, t2_dst, out_dst,
                           free_guard, raw_guard=None, pc=0):
                if rope_tab_prev[0]:
                    kb.wait("sync", dve, rope_tab_prev[0])
                kb.dma("sync", lambda e, s=ropeC[pc]: e.dma_start(out=RPC, in_=s),
                       inc=sd)
                kb.dma("sync", lambda e, s=ropeS[pc]: e.dma_start(out=RPS, in_=s),
                       inc=sd)
                tab_ld = sd.n
                kb.wait("scalar", pe, pe.n)
                if raw_guard is not None and raw_guard[0] is not None:
                    kb.wait("scalar", raw_guard[0], raw_guard[1])
                kb.op("scalar", lambda e: e.activation(
                    raw, dstp, AF.Identity, bias=bias_col, scale=scale), inc=act)
                rev = act.n
                kb.wait("tensor", act, rev)
                lk_acquire("PSF2")
                for nh in range(2):
                    kb.op("tensor", lambda e, nh=nh: e.matmul(
                        PSF2[:, nh * 512:(nh + 1) * 512], PERM,
                        raw[:, nh * 512:(nh + 1) * 512],
                        start=True, stop=True), inc=pe if nh == 1 else None)
                pm = pe.n
                kb.wait("vector", pe, pm)
                kb.wait("vector", sd, tab_ld)
                if free_guard is not None and free_guard[0] is not None:
                    kb.wait("vector", free_guard[0], free_guard[1])
                kb.op("vector", lambda e: e.tensor_mul(t1_dst, raw, RPC), inc=dve)
                kb.op("vector", lambda e: e.tensor_mul(t2_dst, PSF2[:], RPS),
                      inc=dve)
                kb.op("vector", lambda e: e.tensor_add(out_dst, t1_dst, t2_dst),
                      inc=dve)
                rope_tab_prev[0] = dve.n
                lk_release("PSF2", dve, dve.n)
                return dve.n

            kb.wait("scalar", sd, lb1)
            kv1_st = [(None, 0), (None, 0)]
            rope_prev = [0]
            for kind, row0, bcol, rope in (("k", 0, 16, True), ("v", D, 32, False)):
                for pc in range(NFC):
                    i = prep_plain(kind, pc)
                    sp = i % 2
                    dstp, lname = proj_mms(i, i % 2, xnt_rhs)
                    s_, v_ = kv1_st[sp]
                    if rope:
                        ev = rope_apply(dstp, BIAS[:, bcol + pc:bcol + pc + 1],
                                        1.0, PBr1[:, 0:1024], KVS1[sp],
                                        PBr1[:, 1024:2048], KVS1[sp],
                                        (s_, v_) if s_ is not None else None,
                                        raw_guard=(dve, rope_prev[0]) if rope_prev[0] else None,
                                        pc=pc)
                        rope_prev[0] = ev
                        lk_release(lname, act, act.n)
                    else:
                        kb.wait("scalar", pe, pe.n)
                        if s_ is not None:
                            kb.wait("scalar", s_, v_)
                        kb.op("scalar", lambda e, d=KVS1[sp], s2=dstp,
                            b=BIAS[:, bcol + pc:bcol + pc + 1]:
                            e.activation(d, s2, AF.Identity, bias=b), inc=act)
                        ev = act.n
                        lk_release(lname, act, ev)
                    eng_sem = dve if rope else act
                    kb.wait("gpsimd", eng_sem, ev)
                    kb.dma("gpsimd", lambda e,
                        d=kv1_in[row0 + pc * 128:row0 + (pc + 1) * 128, :],
                        s=KVS1[sp]: e.dma_start(out=d, in_=s), inc=gd)
                    kv1_st[sp] = (gd, gd.n)
            cc1 = kb.sem("cc1")
            kb.wait("gpsimd", gd, gd.n)
            kb.op("gpsimd", lambda e: e.collective_compute(
                "AllGather", ALU.bypass,
                replica_groups=[[0, 1], [2, 3], [4, 5], [6, 7]],
                ins=[kv1_in], outs=[kv1_out]), inc=cc1)

            ev_exp = {}
            ev_stt = {}
            ev_ptev = {}
            ev_comb = {}
            ev_qth = {}
            sc_last = {}
            vtr_pe = {}
            ev_vtl = {}
            av_pe_last = [0]
            pend = []

            def emit_tr1(hp16, sub, qc):
                pbuf = PBX1[qc % 2]
                kb.wait("tensor", act, ev_ptev[("norm", hp16, sub, qc)])
                lk_acquire("PSB")
                for kc in range(16):
                    kb.op("tensor", lambda e, kc=kc, pbuf=pbuf:
                        e.transpose(PSB[:, kc * 128:(kc + 1) * 128],
                                    pbuf[:, kc * 128:(kc + 1) * 128], IDB),
                        inc=pe if kc == 15 else None)
                trp = pe.n
                kb.wait("vector", pe, trp)
                if av_pe_last[0]:
                    kb.wait("vector", pe, av_pe_last[0])
                kb.op("vector", lambda e: e.tensor_copy(PTG1, PSB[:]), inc=dve)
                ev_ptev[(hp16, sub, qc)] = (dve.n, trp)
                lk_release("PSB", dve, dve.n)
                kb.wait("tensor", dve, ev_ptev[(hp16, sub, qc)][0])
                kb.wait("tensor", dve, ev_vtl[hp16])
                lk_acquire("PSF2")
                for kc in range(16):
                    kb.op("tensor", lambda e, kc=kc, sub=sub: e.matmul(
                        PSF2[0:64, 0:128],
                        VTL1[:, kc * 128 + sub * 64:kc * 128 + (sub + 1) * 64],
                        PTG1[:, kc * 128:(kc + 1) * 128],
                        start=(kc == 0), stop=(kc == 15)),
                        inc=pe if kc == 15 else None)
                av = pe.n
                av_pe_last[0] = av
                kb.wait("vector", pe, av)
                kb.op("vector", lambda e, fc=hp16, hh=sub, qc=qc: e.tensor_copy(
                    ATT1[hh * 64:(hh + 1) * 64,
                         fc * 1024 + qc * 128:fc * 1024 + (qc + 1) * 128],
                    PSF2[0:64, 0:128]), inc=dve)
                lk_release("PSF2", dve, dve.n)

            for hp16 in range(16):
                par = hp16 % 2
                i = prep_plain("q", hp16)
                dstp, lname = proj_mms(i, i % 2, xnt_rhs)
                fg = (pe, sc_last[hp16 - 2]) if hp16 >= 2 else None
                if hp16 == 0:
                    rg = kv1_st[0]
                elif hp16 == 1:
                    rg = kv1_st[1]
                else:
                    rg = (dve, ev_qth[hp16 - 1])
                rope_apply(dstp, BIAS[:, 64 + hp16:65 + hp16], 0.125,
                           KVS1[0], KVS1[1], PBr1[:, 0:1024], QTH1[par], fg,
                           raw_guard=rg, pc=hp16)
                lk_release(lname, act, act.n)
                ev_qth[hp16] = dve.n

                if hp16 == 0:
                    kb.wait("scalar", cc1, 1)
                else:
                    kb.wait("scalar", pe, sc_last[hp16 - 1])
                    kb.wait("scalar", pe, ev_ptev[(hp16 - 1, 1, NCH - 1)][1])
                src = kv1_out[:, hp16 * 128:(hp16 + 1) * 128, :].rearrange("r p t -> p r t")
                kb.dma("scalar", lambda e, s=src: e.dma_start(
                    out=KTH1[:].rearrange("p (r t) -> p r t", r=2), in_=s), inc=ad)
                srcv = kv1_out[:, D + hp16 * 128:D + (hp16 + 1) * 128, :].rearrange("r p t -> p r t")
                kb.dma("scalar", lambda e, s=srcv: e.dma_start(
                    out=VTR1[:].rearrange("p (r t) -> p r t", r=2), in_=s), inc=ad)
                ld_kv = ad.n

                kb.wait("tensor", ad, ld_kv)
                lk_acquire("PSB")
                for kc in range(16):
                    kb.op("tensor", lambda e, kc=kc:
                        e.transpose(PSB[:, kc * 128:(kc + 1) * 128],
                                    VTR1[:, kc * 128:(kc + 1) * 128], IDB),
                        inc=pe if kc == 15 else None)
                vtr_pe[hp16] = pe.n
                kb.wait("vector", pe, pe.n)
                if av_pe_last[0]:
                    kb.wait("vector", pe, av_pe_last[0])
                kb.op("vector", lambda e: e.tensor_copy(VTL1, PSB[:]), inc=dve)
                ev_vtl[hp16] = dve.n
                lk_release("PSB", dve, dve.n)

                for sub in range(2):
                    for qc in range(NCH):
                        kb.op("vector", lambda e, qc=qc: e.tensor_scalar(
                            MASKQ, IOTAF, THR[:, qc:qc + 1], None, ALU.is_le),
                            inc=dve)
                        mk = dve.n
                        kb.wait("tensor", dve, ev_qth[hp16])
                        if sub == 0 and qc == 0:
                            kb.wait("tensor", ad, ld_kv)
                        for b2 in range(4):
                            prev = (hp16, sub, qc - 1, b2)
                            if qc == 0:
                                prev = (hp16, sub - 1, NCH - 1, b2) if sub else \
                                       (hp16 - 1, 1, NCH - 1, b2)
                            if prev in ev_exp:
                                kb.wait("tensor", act, ev_exp[prev])
                            kb.op("tensor", lambda e, b2=b2, par=par, sub=sub,
                                qc=qc: e.matmul(
                                PSF[:, b2 * 512:(b2 + 1) * 512],
                                QTH1[par][sub * 64:(sub + 1) * 64, qc * 128:(qc + 1) * 128],
                                KTH1[sub * 64:(sub + 1) * 64, b2 * 512:(b2 + 1) * 512],
                                start=True, stop=True), inc=pe)
                            mmb = pe.n
                            kb.wait("scalar", pe, mmb)
                            prevt = (hp16, sub, qc - 2)
                            if qc < 2:
                                prevt = (hp16, sub - 1, NCH - 2 + qc) if sub else \
                                        (hp16 - 1, 1, NCH - 2 + qc)
                            if prevt in ev_ptev and b2 == 0:
                                kb.wait("scalar", pe, ev_ptev[prevt][1])
                            prevc = (hp16, sub, qc - 1)
                            if qc == 0:
                                prevc = (hp16, sub - 1, NCH - 1) if sub else \
                                        (hp16 - 1, 1, NCH - 1)
                            if prevc in ev_stt and b2 == 0:
                                kb.wait("scalar", dve, ev_stt[prevc])
                            kb.op("scalar", lambda e, b2=b2: e.activation(
                                PBr1[:, b2 * 512:(b2 + 1) * 512],
                                PSF[:, b2 * 512:(b2 + 1) * 512], AF.Exp), inc=act)
                            ev_exp[(hp16, sub, qc, b2)] = act.n
                        sc_last[hp16] = pe.n
                        kb.wait("vector", act, ev_exp[(hp16, sub, qc, 3)])
                        if ("normt",) in ev_ptev:
                            kb.wait("vector", act, ev_ptev[("normt",)])
                        kb.op("vector", lambda e, qc=qc: e.scalar_tensor_tensor(
                            out=NORMT, in0=PBr1, scalar=1.0, in1=MASKQ,
                            op0=ALU.bypass, op1=ALU.mult,
                            accum_out=ST[:, 100:101]), inc=dve)
                        ev_stt[(hp16, sub, qc)] = dve.n
                        gap("vector")
                        jj = (hp16 * 2 + sub) * NCH + qc
                        rpar = jj % 2
                        if ("rcol", rpar) in ev_ptev:
                            kb.wait("vector", act, ev_ptev[("rcol", rpar)])
                        kb.op("vector", lambda e, rp=rpar: e.reciprocal(
                            ST[:, 101 + rp:102 + rp], ST[:, 100:101]), inc=dve)
                        rc = dve.n
                        kb.wait("scalar", dve, rc)
                        if prevt in ev_ptev:
                            kb.wait("scalar", pe, ev_ptev[prevt][1])
                        if sub == 0 and qc == 1:
                            kb.wait("scalar", pe, vtr_pe[hp16])
                        kb.op("scalar", lambda e, rp=rpar, pbuf=PBX1[qc % 2]:
                            e.activation(pbuf[:], NORMT, AF.Identity,
                                         scale=ST[:, 101 + rp:102 + rp]), inc=act)
                        ev_ptev[("norm", hp16, sub, qc)] = act.n
                        ev_ptev[("rcol", rpar)] = act.n
                        ev_ptev[("normt",)] = act.n
                        pend.append((hp16, sub, qc))
                        if len(pend) >= 2:
                            emit_tr1(*pend.pop(0))
                    while pend:
                        emit_tr1(*pend.pop(0))

            last_exp = ev_exp[(15, 1, NCH - 1, 3)]
            lk_release("PSF0", act, last_exp)
            lk_release("PSF1", act, last_exp)
            kvo_free = [(None, 0), (None, 0)]
            for pc in range(NFC):
                i = prep_plain("o", pc)
                sp = i % 2
                dstp, lname = proj_mms(i, i % 2, att_rhs)
                kb.wait("scalar", pe, pe.n)
                fs, fv = kvo_free[sp]
                if fs is not None:
                    kb.wait("scalar", fs, fv)
                kb.op("scalar", lambda e, d=KVS1[sp], s2=dstp,
                    b=BIAS[:, 48 + pc:48 + pc + 1]:
                    e.activation(d, s2, AF.Identity, bias=b), inc=act)
                ev = act.n
                lk_release(lname, act, ev)
                kb.wait("tensor", act, ev)
                lk_acquire("PSB")
                for tc in range(NCH):
                    kb.op("tensor", lambda e, tc=tc, sp=sp:
                        e.transpose(PSB[:, tc * 128:(tc + 1) * 128],
                                    KVS1[sp][:, tc * 128:(tc + 1) * 128], IDB),
                        inc=pe if tc == NCH - 1 else None)
                tro = pe.n
                kvo_free[sp] = (pe, tro)
                kb.wait("vector", pe, tro)
                dstx = X[:].rearrange("p (c f) -> p c f", f=D)[:, :, pc * 128:(pc + 1) * 128]
                srcx = PSB[:, 0:1024].rearrange("p (c f) -> p c f", f=128)
                kb.op("vector", lambda e, d=dstx, s=srcx:
                    e.tensor_add(d, d, s), inc=dve)
                lk_release("PSB", dve, dve.n)
            return dict(done_dve=dve.n, done_act=act.n, done_pe=pe.n,
                        done_pool=pool.n)

        ph5 = l1_attn()

        if "att1" in dbg:
            datt1 = outp("dbg_att1", [P, 16 * TOK], BF16)
            kb.wait("sync", dve, ph5["done_dve"])
            kb.dma("sync", lambda e, d=datt1: e.dma_start(out=d, in_=ATT1), inc=sto)

        wg_t = inp("wg_t", [16, 11, P, D])
        wu_t = inp("wu_t", [16, 11, P, D])
        wd_t = inp("wd_t", [16, 11, P, D])
        MOE_M = 11

        ZB = AB16[:, 0:11264]
        WD = AB16[:, 11264:33792]
        WGP = [AB16[:, 33792:35840], AB16[:, 35840:37888]]
        CWB2 = AF32[:, 6144:6272]
        CSTG2 = AF32[:, 0:2048]
        GS = AF32[:, 2048:3072]
        WDST = AF32[:, 3072:5120]
        TMPX2 = AF32[:, 5120:6144]

        def moe():
            for eng in ("sync", "scalar", "vector", "tensor", "gpsimd"):
                kb.wait(eng, dve, ph5["done_dve"])
                kb.wait(eng, act, ph5["done_act"])
                kb.wait(eng, pe, ph5["done_pe"])
                kb.wait(eng, pool, pool.n)
            wg_ready = {}
            wg_done = {}
            wd_ready = {}
            ev_z = {}
            ev_silu = {}
            ev_down = {}
            ev_xup = {}
            stg_free = {}
            wd_cast_read = [0]

            for ex in range(16):
                for mc in range(MOE_M):
                    gi = ex * MOE_M + mc
                    if gi >= 1:
                        kb.wait("gpsimd", pool, wd_ready.get(gi - 1, 0))
                    if ex >= 1 and mc == 0:
                        kb.wait("gpsimd", pe, ev_down[ex - 1])
                    kb.dma("gpsimd", lambda e, s=wd_t[ex, mc]:
                        e.dma_start(out=WDST, in_=s), inc=gd)
                    kb.wait("gpsimd", gd, gd.n)
                    kb.op("gpsimd", lambda e, mc=mc: e.tensor_copy(
                        WD[:, mc * 2048:(mc + 1) * 2048], WDST), inc=pool)
                    wd_ready[gi] = pool.n

                for mc in range(MOE_M):
                    for kind, wt in (("g", wg_t), ("u", wu_t)):
                        gi2 = (ex * MOE_M + mc) * 2 + (0 if kind == "g" else 1)
                        par = gi2 % 2
                        if gi2 >= 1:
                            kb.wait("sync", dve, wg_ready[gi2 - 1])
                        kb.dma("sync", lambda e, s=wt[ex, mc]:
                            e.dma_start(out=CSTG2, in_=s), inc=sd)
                        ldn = sd.n
                        kb.wait("vector", sd, ldn)
                        if gi2 >= 2:
                            kb.wait("vector", pe, wg_done.get(gi2 - 2, 0))
                        kb.op("vector", lambda e, par=par: e.tensor_copy(
                            WGP[par], CSTG2), inc=dve)
                        wg_ready[gi2] = dve.n
                        kb.wait("tensor", dve, wg_ready[gi2])
                        if kind == "g":
                            lk_acquire("PSF2")
                            dstp = PSF2[:]
                        else:
                            lk_acquire("PSF0")
                            dstp = PSF[:, 0:1024]
                        for dc in range(NFC):
                            for nh in range(2):
                                kb.op("tensor", lambda e, dc=dc, nh=nh, par=par,
                                    d=dstp[:, nh * 512:(nh + 1) * 512]:
                                    e.matmul(d, WGP[par][:, dc * 128:(dc + 1) * 128],
                                             XNT[:, dc * 1024 + nh * 512:dc * 1024 + (nh + 1) * 512],
                                             start=(dc == 0), stop=(dc == NFC - 1)),
                                    inc=pe if (dc == NFC - 1 and nh == 1) else None)
                        wg_done[gi2] = pe.n
                        if kind == "g":
                            kb.wait("scalar", pe, pe.n)
                            prev_z = ev_z.get((ex, mc - 1)) if mc >= 1 else \
                                (ev_z.get((ex - 1, MOE_M - 1)) if ex >= 1 else None)
                            if prev_z:
                                kb.wait("scalar", dve, prev_z)
                            kb.op("scalar", lambda e: e.activation(
                                GS, PSF2[:], AF.Silu), inc=act)
                            ev_silu[(ex, mc)] = act.n
                            lk_release("PSF2", act, act.n)
                        else:
                            kb.wait("vector", pe, pe.n)
                            kb.wait("vector", act, ev_silu[(ex, mc)])
                            if ex >= 1 and mc == 0:
                                kb.wait("vector", pe, ev_down[ex - 1])
                            kb.op("vector", lambda e, mc=mc: e.tensor_mul(
                                ZB[:, mc * 1024:(mc + 1) * 1024], GS,
                                PSF[:, 0:1024]), inc=dve)
                            ev_z[(ex, mc)] = dve.n
                            lk_release("PSF0", dve, dve.n)

                kb.wait("tensor", dve, ev_z[(ex, MOE_M - 1)])
                kb.wait("tensor", pool, wd_ready[ex * MOE_M + MOE_M - 1])
                lk_acquire("PSF0")
                lk_acquire("PSF1")
                for tc in range(NCH):
                    if (ex, tc - 1) in ev_xup:
                        kb.wait("tensor", dve, ev_xup[(ex, tc - 1)])
                    elif ex >= 1 and (ex - 1, NCH - 1) in ev_xup:
                        kb.wait("tensor", dve, ev_xup[(ex - 1, NCH - 1)])
                    for mc in range(MOE_M):
                        for nh in range(4):
                            kb.op("tensor", lambda e, mc=mc, nh=nh, tc=tc:
                                e.matmul(PSF[:, nh * 512:(nh + 1) * 512],
                                         ZB[:, mc * 1024 + tc * 128:mc * 1024 + (tc + 1) * 128],
                                         WD[:, mc * 2048 + nh * 512:mc * 2048 + (nh + 1) * 512],
                                         start=(mc == 0), stop=(mc == MOE_M - 1)),
                                inc=pe if (mc == MOE_M - 1 and nh == 3) else None)
                    dn = pe.n
                    kb.wait("vector", pe, dn)
                    for hf in range(2):
                        kb.op("vector", lambda e, tc=tc, hf=hf, ex=ex:
                            e.tensor_scalar(
                            TMPX2, PSF[:, hf * 1024:(hf + 1) * 1024],
                            CWB2[:, tc * 16 + ex:tc * 16 + ex + 1], None,
                            ALU.mult), inc=dve)
                        kb.op("vector", lambda e, tc=tc, hf=hf: e.tensor_add(
                            X[:, tc * D + hf * 1024:tc * D + (hf + 1) * 1024],
                            X[:, tc * D + hf * 1024:tc * D + (hf + 1) * 1024],
                            TMPX2), inc=dve)
                    ev_xup[(ex, tc)] = dve.n
                ev_down[ex] = pe.n
                lk_release("PSF0", dve, dve.n)
                lk_release("PSF1", dve, dve.n)
            return dict(done_dve=dve.n, done_act=act.n, done_pe=pe.n,
                        done_pool=pool.n)

        ph6 = moe()

        if "cw" in dbg:
            dcw = outp("dbg_cw", [P, 128], BF16)
            kb.wait("sync", dve, ph4["done_dve"])
            kb.dma("sync", lambda e, d=dcw: e.dma_start(out=d, in_=CWB), inc=sto)



        if "st" in dbg:
            dst_ = outp("dbg_st", [P, 160], F32)
            kb.wait("sync", dve, dve.n)
            kb.wait("sync", act, act.n)
            kb.dma("sync", lambda e: e.dma_start(out=dst_, in_=ST[:]), inc=sto)
        if "xnt" in dbg:
            dxnt = outp("dbg_xnt", [P, NFC * TOK], BF16)
            kb.wait("sync", dve, ln0["done_dve"])
            kb.dma("sync", lambda e: e.dma_start(out=dxnt, in_=XNT[:]), inc=sto)

        kb.wait("sync", dve, dve.n)
        kb.wait("sync", act, act.n)
        kb.wait("sync", pe, pe.n)
        if pool.n:
            kb.wait("sync", pool, pool.n)
        kb.dma("sync", lambda e: e.dma_start(
            out=out_x.rearrange("(c p) f -> p c f", p=P),
            in_=X[:].rearrange("p (c f) -> p c f", f=D)), inc=sto)
        kb.wait("sync", sto, sto.n)

        with nc.Block() as block:
            kb.finalize(block)

    return nc, io


def _consts_host():
    cf = np.zeros((P, 640), np.float32)
    cf[:, 0:128] = np.eye(P, dtype=np.float32)
    cf[:, 320:448] = 1.0
    cf[:, 576] = EPS
    cb = np.zeros((P, 512), np.float32)
    cb[:, 0:128] = np.eye(P, dtype=np.float32)
    perm = np.zeros((P, P), np.float32)
    sw = np.concatenate([np.arange(32, 64), np.arange(0, 32),
                         np.arange(96, 128), np.arange(64, 96)])
    perm[sw, np.arange(P)] = 1.0
    cb[:, 128:256] = perm.T
    cb[:, 256:384] = 1.0
    for b in range(4):
        cb[b, 384 + b * 32:384 + (b + 1) * 32] = 0.0
    jj = np.arange(128)
    for b in range(4):
        cb[b, 384:512] = (jj // 32 == b).astype(np.float32)
    return cf, cb.astype(ml_dtypes.bfloat16)


def _scale_pieces(scales):
    s = np.asarray(scales, np.float32).reshape(16, 4, 16, 128)
    return np.ascontiguousarray(s.transpose(2, 1, 0, 3).reshape(16, 4, 2048))


def _bias_cols(bias):
    return np.ascontiguousarray(np.asarray(bias, np.float32).reshape(16, P).T)


def make_in_maps(inputs):
    x = np.ascontiguousarray(np.asarray(inputs["L0_attn_x"], np.float32))
    cf, cb = _consts_host()
    f32c = lambda a: np.ascontiguousarray(np.asarray(a, np.float32))
    shared = {
        "consts": cf,
        "constb": cb,
        "L0q_codes": f32c(inputs["L0_attn_wq_codes"]),
        "L0k_codes": f32c(inputs["L0_attn_wk_codes"]),
        "L0v_codes": f32c(inputs["L0_attn_wv_codes"]),
        "L0o_codes": f32c(inputs["L0_attn_wo_codes"]),
        "L0q_scp": _scale_pieces(inputs["L0_attn_wq_scales"]),
        "L0k_scp": _scale_pieces(inputs["L0_attn_wk_scales"]),
        "L0v_scp": _scale_pieces(inputs["L0_attn_wv_scales"]),
        "L0o_scp": _scale_pieces(inputs["L0_attn_wo_scales"]),
        "L0_bias": np.concatenate(
            [_bias_cols(inputs["L0_attn_" + k]) for k in ("bq", "bk", "bv", "bo")],
            axis=1),
        "cfc_t": np.ascontiguousarray(
            np.asarray(inputs["L0_ffn_c_fc"], np.float32)
            .reshape(D, 16, 512).transpose(1, 0, 2)),
        "cpj_t": np.ascontiguousarray(
            np.asarray(inputs["L0_ffn_c_proj"], np.float32)
            .T.reshape(64, P, D)),
        "gwT2": np.ascontiguousarray(
            np.asarray(inputs["L1_ffn_gate_w"], np.float32)
            .reshape(16, 16, P).transpose(2, 1, 0).reshape(P, 256)),
        "L1_wqT": np.ascontiguousarray(np.asarray(inputs["L1_attn_wq_w"], np.float32).T),
        "L1_wkT": np.ascontiguousarray(np.asarray(inputs["L1_attn_wk_w"], np.float32).T),
        "L1_wvT": np.ascontiguousarray(np.asarray(inputs["L1_attn_wv_w"], np.float32).T),
        "L1_woT": np.ascontiguousarray(np.asarray(inputs["L1_attn_wo_w"], np.float32).T),
        "L1_bias": np.concatenate(
            [_bias_cols(inputs["L1_attn_" + k]) for k in ("wq_b", "wk_b", "wv_b", "wo_b")],
            axis=1),
        "iotaf": np.ascontiguousarray(
            np.tile(np.arange(D, dtype=np.float32)[None, :], (P, 1))),
        "wg_t": np.ascontiguousarray(
            np.asarray(inputs["L1_ffn_w_gate"], np.float32)
            .reshape(16, 11, P, 16, P).transpose(0, 1, 4, 3, 2)
            .reshape(16, 11, P, D)),
        "wu_t": np.ascontiguousarray(
            np.asarray(inputs["L1_ffn_w_up"], np.float32)
            .reshape(16, 11, P, 16, P).transpose(0, 1, 4, 3, 2)
            .reshape(16, 11, P, D)),
        "wd_t": np.ascontiguousarray(
            np.asarray(inputs["L1_ffn_w_down"], np.float32)
            .transpose(0, 2, 1).reshape(16, 11, P, D)),
    }
    tsv = 10000.0 ** (np.arange(32, dtype=np.float64) / 32.0)

    def _rope_tabs(h):
        pos = np.arange(TOK, dtype=np.float64) + h * TOK
        C = np.zeros((16, P, TOK), np.float32)
        S = np.zeros((16, P, TOK), np.float32)
        for fc in range(16):
            for sub in range(2):
                hd = 2 * fc + sub
                ang = pos / tsv[hd]
                c = np.cos(ang).astype(np.float32)
                s = np.sin(ang).astype(np.float32)
                r0 = sub * 64
                C[fc, r0:r0 + 64] = c[None, :]
                S[fc, r0:r0 + 32] = -s[None, :]
                S[fc, r0 + 32:r0 + 64] = s[None, :]
        return np.ascontiguousarray(C), np.ascontiguousarray(S)
    maps = []
    for c in range(8):
        b, h = c // 2, c % 2
        m = dict(shared)
        m["xs"] = np.ascontiguousarray(x[b, h * TOK:(h + 1) * TOK, :])
        Cc, Ss = _rope_tabs(h)
        m["ropeC"] = Cc
        m["ropeS"] = Ss
        thr = (h * TOK + np.arange(NCH)[None, :] * P
               + np.arange(P)[:, None]).astype(np.float32)
        m["thr"] = np.ascontiguousarray(thr)
        maps.append(m)
    return maps


def kernel(**inputs):
    nc, _ = build_core()
    maps = make_in_maps(inputs)
    res = run_bass_kernel_spmd(nc, maps, core_ids=list(range(8)))
    out = np.zeros((B, S, D), np.float32)
    for c in range(8):
        b, h = c // 2, c % 2
        out[b, h * TOK:(h + 1) * TOK, :] = res.results[c]["out_x"]
    return out
